# version 14
# speedup vs baseline: 1.2184x; 1.0017x over previous
import numpy as np
from contextlib import ExitStack, contextmanager
import concourse.bass as bass
import concourse.mybir as mybir
from concourse.bass_utils import run_bass_kernel_spmd
from concourse.bass_core_rust import IndirectOffsetOnAxis

F32 = mybir.dt.float32
BF16 = mybir.dt.bfloat16
AF = mybir.ActivationFunctionType
ALU = mybir.AluOpType
AX = mybir.AxisListType

NCORES = 8
L = 4096
D = 1024
ENGINES = ("tensor", "vector", "scalar", "gpsimd", "sync")


class Ctx:

    NCONST = 32

    def __init__(self, nc, stack):
        self.nc = nc
        self.stack = stack
        self.sem_stack = stack
        self.q = {e: [] for e in ENGINES}
        self.sems = {}
        self.cnt = {}
        self._cms = []
        self.constv = []
        self._ccol = None

    @contextmanager
    def scope(self):
        nc = self.nc
        sb, st_, pb, pt = nc.sbuf_base, nc.sbuf_top, nc.psum_base, nc.psum_top
        try:
            yield
        finally:
            nc.sbuf_base, nc.sbuf_top = sb, st_
            nc.psum_base, nc.psum_top = pb, pt

    def phase_barrier(self):
        for e in ENGINES:
            self.op(e, lambda q: q.nop(), inc=(("gbar", 1),))
        v = self.val("gbar")
        for e in ENGINES:
            self.wait(e, ("gbar", v))

    def ccol(self, value):
        value = float(value)
        if self._ccol is None:
            self._ccol = self.sbuf("ccol", [128, self.NCONST], F32)
        if value not in self.constv:
            assert len(self.constv) < self.NCONST
            self.constv.append(value)
        i = self.constv.index(value)
        return self._ccol[:, i:i + 1]

    def emit_const_load(self, cc_hbm):
        if self._ccol is None:
            self._ccol = self.sbuf("ccol", [128, self.NCONST], F32)
        self.op("sync", lambda y: y.dma_start(out=self._ccol[:, :], in_=cc_hbm[:, :]),
                inc=(("cst", 16),))

    def const_array(self):
        vals = np.zeros((self.NCONST,), dtype=np.float32)
        vals[:len(self.constv)] = self.constv
        return np.tile(vals[None, :], (128, 1))

    def sem(self, name):
        if name not in self.sems:
            cm = self.nc.semaphore(name)
            self.sems[name] = cm.__enter__()
            self._cms.append(cm)
            self.cnt[name] = 0
        return self.sems[name]

    def val(self, name):
        self.sem(name)
        return self.cnt[name]

    def bump(self, name, by):
        self.sem(name)
        self.cnt[name] += by
        return self.cnt[name]

    def sbuf(self, name, shape, dtype):
        cm = self.nc.sbuf_tensor(name, shape, dtype)
        h = cm.__enter__()
        self._cms.append(cm)
        return h

    def psum(self, name, shape, dtype=F32):
        cm = self.nc.psum_tensor(name, shape, dtype)
        h = cm.__enter__()
        self._cms.append(cm)
        return h

    def op(self, eng, build, wait=(), inc=()):
        incs = []
        for s, a in inc:
            self.bump(s, a)
            incs.append((self.sems[s], a))
        waits = [(self.sem(s), v) for s, v in wait]

        def run(e):
            for h, v in waits:
                e.wait_ge(h, v)
            ins = build(e) if build is not None else None
            if ins is not None and incs:
                ins.then_inc(*incs[0])
                rest = incs[1:]
            else:
                rest = incs
            for h, a in rest:
                e.nop().then_inc(h, a)
        self.q[eng].append(run)

    def wait(self, eng, *waits):
        self.op(eng, None, wait=waits)

    def assemble(self):
        nc = self.nc
        with nc.Block() as block:
            @block.tensor
            def _(e):
                for f in self.q["tensor"]:
                    f(e)

            @block.vector
            def _(e):
                for f in self.q["vector"]:
                    f(e)

            @block.scalar
            def _(e):
                for f in self.q["scalar"]:
                    f(e)

            @block.gpsimd
            def _(e):
                for f in self.q["gpsimd"]:
                    f(e)

            @block.sync
            def _(e):
                for f in self.q["sync"]:
                    f(e)

    def arena(self, name, bytes_per_partition):
        return self.sbuf(name, [128, bytes_per_partition], mybir.dt.uint8)


def asap(x):
    return x if not hasattr(x, "ap") or not callable(getattr(x, "ap", None)) else x.ap()


def carve(arena_handle, byte_off, cols, dtype, parts=128):
    sz = mybir.dt.size(dtype)
    assert byte_off % 32 == 0
    h = arena_handle.bitcast(dtype)
    start = byte_off // sz
    return h.ap()[0:parts, start:start + cols]


class PsumPool:

    def __init__(self, ctx, tag, n=8, shape=(128, 512)):
        self.ctx = ctx
        self.banks = [ctx.psum(f"{tag}_pp{i}", list(shape), F32) for i in range(n)]
        self.free_at = [0] * n
        self.rr = 0
        self.mm = tag + "_pmm"
        self.ev = tag + "_pev"

    def group(self, mm_builders, evac_builder, evac_eng="vector",
              first_wait=(), evac_wait=()):
        ctx = self.ctx
        i = self.rr % len(self.banks)
        self.rr += 1
        bank = self.banks[i]
        w = list(first_wait)
        if self.free_at[i] > 0:
            w.append((self.ev, self.free_at[i]))
        n = len(mm_builders)
        for j, mb in enumerate(mm_builders):
            ctx.op("tensor", (lambda t, mb=mb, bank=bank: mb(t, bank)),
                   wait=w if j == 0 else (), inc=((self.mm, 1),) if j == n - 1 else ())
        ctx.op(evac_eng, (lambda e, bank=bank: evac_builder(e, bank)),
               wait=[(self.mm, ctx.val(self.mm))] + list(evac_wait),
               inc=((self.ev, 1),))
        self.free_at[i] = ctx.val(self.ev)
        return bank


def host_constants():
    import ml_dtypes
    c = {}
    c["cid_bf"] = np.eye(128, dtype=ml_dtypes.bfloat16)
    c["cid_f32"] = np.eye(128, dtype=np.float32)
    c["onesrow"] = np.ones((1, 128), dtype=ml_dtypes.bfloat16)
    h = np.arange(16)
    c["c16x8"] = (h[:, None] // 2 == np.arange(8)[None, :]).astype(np.float32)
    p = np.arange(128)
    c["cB16"] = (p[None, :] // 64 == (h[:, None] % 2)).astype(np.float32)
    c["ones64"] = np.ones((64, 64), dtype=np.float32)
    return c


def col_layout(vec, parts=128):
    vec = np.asarray(vec)
    n = vec.shape[0]
    assert n % parts == 0
    return np.ascontiguousarray(vec.reshape(n // parts, parts).T)


def emit_ln_fm(ctx, x_tm, xn_fm, cid_bf, tag, nchunks=8, also_tm=None,
               also_tmb=None):
    nc = ctx.nc
    CT = L // nchunks
    NST = CT // 128
    ND = D // 128
    ld, cm, tr, ev, st = (tag + s for s in ("_ld", "_cm", "_tr", "_ev", "_st"))

    xt = ctx.sbuf(tag + "_xt", [128, NST * D], F32)
    xnf = ctx.sbuf(tag + "_xnf", [128, NST * D], F32)
    xnb = ctx.sbuf(tag + "_xnb", [128, NST * D], BF16)
    fmb = ctx.sbuf(tag + "_fmb", [128, ND * CT], BF16)
    sq = ctx.sbuf(tag + "_sq", [128, D], F32)
    cols = ctx.sbuf(tag + "_cols", [128, 8], F32)
    idb = ctx.sbuf(tag + "_idb", [128, 128], BF16)
    pt = [ctx.psum(tag + f"_pt{i}", [128, 128], BF16) for i in range(2)]

    ctx.op("sync", lambda g: g.dma_start(out=idb[:, :], in_=cid_bf[:, :]),
           inc=((ld, 16),))

    for c in range(nchunks):
        src = x_tm.rearrange("(c s p) d -> c p s d", c=nchunks, s=NST, p=128)
        xt3 = xt.ap().rearrange("p (s d) -> p s d", s=NST)
        ctx.op("sync", lambda y, c=c: y.dma_start(out=xt3, in_=src[c]),
               wait=((st, ctx.val(st)),),
               inc=((ld, 16),))
        ldv = ctx.val(ld)
        for s in range(NST):
            tile = xt[:, s * D:(s + 1) * D]
            sumc = cols[:, s:s + 1]
            sqc = cols[:, 4 + s:4 + s + 1]
            ctx.op("scalar", lambda e, tile=tile, sumc=sumc:
                   e.activation(sq[:, :], tile, AF.Copy, accum_out=sumc),
                   wait=((ld, ldv),) if s == 0 else (), inc=((cm, 1),))
            ctx.op("scalar", lambda e, tile=tile, sqc=sqc:
                   e.activation(sq[:, :], tile, AF.Square, accum_out=sqc),
                   inc=((cm, 1),))
        cmv = ctx.val(cm)
        m = cols[:, 0:4]
        msq = cols[:, 4:8]
        ctx.op("vector", lambda v: v.tensor_scalar(m, m, 1.0 / D, None, ALU.mult),
               wait=((cm, cmv),), inc=((cm, 1),))
        ctx.op("vector", lambda v: v.tensor_scalar(msq, msq, 1.0 / D, None, ALU.mult),
               inc=((cm, 1),))
        ctx.wait("vector", (cm, ctx.val(cm)))
        ctx.op("vector", lambda v: v.tensor_mul(sq[:, 0:4], m, m), inc=((cm, 1),))
        ctx.wait("vector", (cm, ctx.val(cm)))
        ctx.op("vector", lambda v: v.tensor_sub(msq, msq, sq[:, 0:4]), inc=((cm, 1),))
        ctx.wait("scalar", (cm, ctx.val(cm)))
        eps = ctx.ccol(1e-5)
        ctx.op("scalar", lambda e: e.activation(msq, msq, AF.Ln, bias=eps),
               wait=(("cst", 16),), inc=((cm, 1),))
        ctx.wait("scalar", (cm, ctx.val(cm)))
        ctx.op("scalar", lambda e: e.activation(msq, msq, AF.Exp, scale=-0.5),
               inc=((cm, 1),))
        cmv2 = ctx.val(cm)
        for s in range(NST):
            tile = xt[:, s * D:(s + 1) * D]
            ctx.op("vector", lambda v, s=s, tile=tile: v.tensor_scalar(
                xnf[:, s * D:(s + 1) * D], tile, m[:, s:s + 1], msq[:, s:s + 1],
                ALU.subtract, ALU.mult),
                wait=((cm, cmv2),) if s == 0 else (), inc=((cm, 1),))
        ctx.wait("scalar", (cm, ctx.val(cm)))
        ctx.wait("scalar", (tr, ctx.val(tr)))
        for s in range(NST):
            ctx.op("scalar", lambda e, s=s: e.activation(
                xnb[:, s * D:(s + 1) * D], xnf[:, s * D:(s + 1) * D], AF.Copy),
                inc=((cm, 1),))
        if also_tm is not None:
            dstt = also_tm.rearrange("(c s p) d -> c p s d", c=nchunks, s=NST, p=128)
            xnf3 = xnf.ap().rearrange("p (s d) -> p s d", s=NST)
            ctx.op("sync", lambda y, c=c: y.dma_start(out=dstt[c], in_=xnf3),
                   wait=((cm, ctx.val(cm)),), inc=((st, 16),))
        if also_tmb is not None:
            dstb = also_tmb.rearrange("(c s p) d -> c p s d", c=nchunks, s=NST, p=128)
            xnb3 = xnb.ap().rearrange("p (s d) -> p s d", s=NST)
            ctx.op("sync", lambda y, c=c: y.dma_start(out=dstb[c], in_=xnb3),
                   wait=((cm, ctx.val(cm)),), inc=((st, 16),))
        cmv3 = ctx.val(cm)
        ti = 0
        for dc in range(ND):
            for s in range(NST):
                bank = pt[ti % 2]
                g = ctx.val(ev)
                w = ([(cm, cmv3)] if ti == 0 else []) + ([(ev, g - 1)] if g >= 1 else [])
                ctx.op("tensor", lambda t, bank=bank, dc=dc, s=s:
                       t.transpose(bank[:, :], xnb[:, s * D + dc * 128: s * D + dc * 128 + 128], idb[:, :]),
                       wait=[(ld, 16)] + w, inc=((tr, 1),))
                ctx.op("vector", lambda v, bank=bank, dc=dc, s=s: v.tensor_copy(
                    fmb[:, dc * CT + s * 128: dc * CT + s * 128 + 128], bank[:, :]),
                    wait=((tr, ctx.val(tr)),), inc=((ev, 1),))
                ti += 1
        dst = xn_fm.rearrange("(dc p) (c t) -> c p dc t", dc=ND, p=128, c=nchunks, t=CT)
        fmb3 = fmb.ap().rearrange("p (dc t) -> p dc t", dc=ND)
        ctx.op("sync", lambda y, c=c, dst=dst: y.dma_start(out=dst[c], in_=fmb3),
               wait=((ev, ctx.val(ev)),), inc=((st, 16),))
    ctx.wait("sync", (st, ctx.val(st)))


def emit_load_w_nat(ctx, tag, w_hbm, Din, Dout, sem=None):
    KC = Din // 128
    wsb = ctx.sbuf(tag + "_w", [128, KC * Dout], BF16)
    src = w_hbm.rearrange("(kc p) dout -> p kc dout", p=128)
    dst = wsb.ap().rearrange("p (kc dout) -> p kc dout", kc=KC)
    sem = sem or (tag + "_ld")
    ctx.op("gpsimd", lambda g: g.dma_start(out=dst, in_=src), inc=((sem, 16),))
    return wsb


def emit_qkv(ctx, tag, xn_fm, wq, wk, wv, bq_c, bk_c, lqk_hbm, c16x8,
             cB16, onesrow, bv_row, q_fm, k_fm, v_tm):
    nc = ctx.nc
    KC, MC = 8, 8
    CT = 512
    NCH = L // CT
    ld, ldg, xnl, mm, ev, st = (tag + s for s in ("_ld", "_ldg", "_xn", "_mm", "_ev", "_st"))

    wq_sb = emit_load_w_nat(ctx, tag + "q", wq, D, D, sem=ldg)
    wk_sb = emit_load_w_nat(ctx, tag + "k", wk, D, D, sem=ldg)
    wv_sb = emit_load_w_nat(ctx, tag + "v", wv, D, D, sem=ldg)
    bqc = ctx.sbuf(tag + "_bqc", [128, 8], F32)
    bkc = ctx.sbuf(tag + "_bkc", [128, 8], F32)
    lqk = ctx.sbuf(tag + "_lqk", [16, 4 * 64], F32)
    m16 = ctx.sbuf(tag + "_m16", [16, 8 + 128], F32)
    o1r = ctx.sbuf(tag + "_o1r", [1, 128], BF16)
    bvr = ctx.sbuf(tag + "_bvr", [1, 1024], BF16)
    lamc = ctx.sbuf(tag + "_lam", [16, 160], F32)
    lamrep = ctx.sbuf(tag + "_lrep", [128, 8], F32)
    bqlam = ctx.sbuf(tag + "_bql", [128, 8], F32)
    xsb = ctx.sbuf(tag + "_x", [128, KC * CT], BF16)
    obuf = ctx.sbuf(tag + "_ob", [128, MC * CT], BF16)
    obuf2 = ctx.sbuf(tag + "_ob2", [128, MC * CT], BF16)
    vbuf = ctx.sbuf(tag + "_vb", [128, 4 * D], BF16)
    ps = [ctx.psum(tag + f"_ps{i}", [128, 512], F32) for i in range(6)]
    pl = ctx.psum(tag + "_pl", [128, 8], F32)

    ctx.op("sync", lambda y: y.dma_start(out=bqc[:, :], in_=bq_c[:, :]), inc=((ld, 16),))
    ctx.op("sync", lambda y: y.dma_start(out=bkc[:, :], in_=bk_c[:, :]), inc=((ld, 16),))
    ctx.op("sync", lambda y: y.dma_start(out=lqk[:, :], in_=lqk_hbm[:, :]), inc=((ld, 16),))
    ctx.op("sync", lambda y: y.dma_start(out=m16[:, 0:8], in_=c16x8[:, :]), inc=((ld, 16),))
    ctx.op("sync", lambda y: y.dma_start(out=m16[:, 8:136], in_=cB16[:, :]), inc=((ld, 16),))
    ctx.op("sync", lambda g: g.dma_start(out=o1r[:, :], in_=onesrow[:, :]), inc=((ld, 16),))
    ctx.op("gpsimd", lambda g: g.dma_start(out=bvr[:, :], in_=bv_row[:, :]), inc=((ldg, 16),))
    ldw = ctx.val(ld)
    ldgw = ctx.val(ldg)

    ctx.op("vector", lambda v: v.tensor_mul(lamc[:, 0:64], lqk[:, 0:64], lqk[:, 64:128]),
           wait=((ld, ldw), (ldg, ldgw)), inc=((ev, 1),))
    ctx.op("vector", lambda v: v.tensor_mul(lamc[:, 64:128], lqk[:, 128:192], lqk[:, 192:256]),
           inc=((ev, 1),))
    ctx.wait("vector", (ev, ctx.val(ev)))
    ctx.op("vector", lambda v: v.reduce_sum(lamc[:, 128:129], lamc[:, 0:64], axis=AX.X), inc=((ev, 1),))
    ctx.op("vector", lambda v: v.reduce_sum(lamc[:, 129:130], lamc[:, 64:128], axis=AX.X), inc=((ev, 1),))
    ctx.wait("scalar", (ev, ctx.val(ev)))
    ctx.op("scalar", lambda e: e.activation(lamc[:, 130:131], lamc[:, 128:129], AF.Exp), inc=((ev, 1),))
    ctx.op("scalar", lambda e: e.activation(lamc[:, 131:132], lamc[:, 129:130], AF.Exp), inc=((ev, 1),))
    ctx.wait("vector", (ev, ctx.val(ev)))
    ctx.op("vector", lambda v: v.tensor_sub(lamc[:, 132:133], lamc[:, 130:131], lamc[:, 131:132]),
           inc=((ev, 1),))
    ctx.wait("vector", (ev, ctx.val(ev)))
    ctx.op("vector", lambda v: v.tensor_scalar(lamc[:, 133:134], lamc[:, 132:133], 0.8, 0.125,
                                               ALU.add, ALU.mult), inc=((ev, 1),))
    ctx.wait("vector", (ev, ctx.val(ev)))
    ctx.op("vector", lambda v: v.tensor_scalar(lamc[:, 0:8], m16[:, 0:8], lamc[:, 133:134], None,
                                               ALU.mult), inc=((ev, 1),))
    ctx.wait("tensor", (ev, ctx.val(ev)))
    ctx.op("tensor", lambda t: t.matmul(pl[:, :], m16[:, 8:136], lamc[:, 0:8],
                                        start=True, stop=True), inc=((mm, 1),))
    ctx.op("vector", lambda v: v.tensor_copy(lamrep[:, :], pl[:, :]),
           wait=((mm, ctx.val(mm)),), inc=((ev, 1),))
    ctx.wait("vector", (ev, ctx.val(ev)))
    ctx.op("vector", lambda v: v.tensor_mul(bqlam[:, :], bqc[:, :], lamrep[:, :]), inc=((ev, 1),))
    lam_ready = ctx.val(ev)
    ctx.wait("scalar", (ev, lam_ready))

    xsrc = xn_fm.rearrange("(kc p) (c t) -> c p kc t", p=128, c=NCH)
    xdst = xsb.ap().rearrange("p (kc t) -> p kc t", kc=KC)
    qdst = q_fm.rearrange("(mc p) (c t) -> c p mc t", p=128, c=NCH)
    kdst = k_fm.rearrange("(mc p) (c t) -> c p mc t", p=128, c=NCH)
    vdst = v_tm.rearrange("(c s p) d -> c p s d", c=NCH, s=4, p=128)
    osrc = obuf.ap().rearrange("p (mc t) -> p mc t", mc=MC)
    osrc2 = obuf2.ap().rearrange("p (mc t) -> p mc t", mc=MC)
    vsrc = vbuf.ap().rearrange("p (s d) -> p s d", s=4)

    grp = 0
    for c in range(NCH):
        ctx.op("sync", lambda y, c=c: y.dma_start(out=xdst, in_=xsrc[c]),
               wait=((mm, ctx.val(mm)),), inc=((xnl, 16),))
        xv = ctx.val(xnl)
        for which, wsb_, ob, bias_ap, scale_ap in (
                ("q", wq_sb, obuf, bqlam, lamrep), ("k", wk_sb, obuf2, bkc, None)):
            for mc in range(MC):
                bank = ps[grp % 6]
                w = [(xnl, xv), (ld, ldw), (ldg, ldgw)]
                if grp >= 6:
                    w.append((ev, lam_ready + grp - 5))
                for kc in range(KC):
                    ctx.op("tensor", lambda t, wsb_=wsb_, bank=bank, kc=kc, mc=mc:
                           t.matmul(bank[:, :], wsb_[:, kc * D + mc * 128: kc * D + mc * 128 + 128],
                                    xsb[:, kc * CT:(kc + 1) * CT],
                                    start=(kc == 0), stop=(kc == KC - 1)),
                           wait=w if kc == 0 else (), inc=((mm, 1),) if kc == KC - 1 else ())
                bias_col = bias_ap[:, mc:mc + 1]
                sc = scale_ap[:, mc:mc + 1] if scale_ap is not None else 1.0
                ctx.op("scalar", lambda e, bank=bank, ob=ob, mc=mc, bias_col=bias_col, sc=sc:
                       e.activation(ob[:, mc * CT:(mc + 1) * CT], bank[:, :], AF.Identity,
                                    bias=bias_col, scale=sc),
                       wait=((mm, ctx.val(mm)),), inc=((ev, 1),))
                grp += 1
            dstq = qdst if which == "q" else kdst
            srcq = osrc if which == "q" else osrc2
            ctx.op("sync", lambda y, c=c, dstq=dstq, srcq=srcq: y.dma_start(out=dstq[c], in_=srcq),
                   wait=((ev, ctx.val(ev)),), inc=((st, 16),))
        for mt in range(4):
            for dh in range(2):
                bank = ps[grp % 6]
                w = [(xnl, xv), (ld, ldw), (ldg, ldgw)]
                if grp >= 6:
                    w.append((ev, lam_ready + grp - 5))
                for kc in range(KC):
                    ctx.op("tensor", lambda t, bank=bank, kc=kc, mt=mt, dh=dh:
                           t.matmul(bank[:, :], xsb[:, kc * CT + mt * 128: kc * CT + mt * 128 + 128],
                                    wv_sb[:, kc * D + dh * 512: kc * D + (dh + 1) * 512],
                                    start=(kc == 0), stop=False),
                           wait=w if kc == 0 else ())
                ctx.op("tensor", lambda t, bank=bank, dh=dh:
                       t.matmul(bank[:, :], o1r[:, :], bvr[:, dh * 512:(dh + 1) * 512],
                                start=False, stop=True), inc=((mm, 1),))
                ctx.op("vector", lambda v, bank=bank, mt=mt, dh=dh:
                       v.tensor_copy(vbuf[:, mt * D + dh * 512: mt * D + (dh + 1) * 512], bank[:, :]),
                       wait=((mm, ctx.val(mm)),), inc=((ev, 1),))
                grp += 1
        ctx.op("sync", lambda y, c=c: y.dma_start(out=vdst[c], in_=vsrc),
               wait=((ev, ctx.val(ev)),), inc=((st, 16),))
    ctx.wait("sync", (st, ctx.val(st)))


def emit_proj_tm(ctx, tag, act_fm, w_hbm, bias_row_hbm, out_tm, Din, Dout,
                 onesrow, out_dtype=F32):
    KC = Din // 128
    ND2 = Dout // 512
    CT = 512
    NCH = L // CT
    ld, xnl, mm, ev, st = (tag + s for s in ("_ld", "_xn", "_mm", "_ev", "_st"))
    wsb = emit_load_w_nat(ctx, tag, w_hbm, Din, Dout, sem=ld)
    o1r = ctx.sbuf(tag + "_o1r", [1, 128], BF16)
    brow = ctx.sbuf(tag + "_br", [1, Dout], BF16)
    xsb = ctx.sbuf(tag + "_x", [128, KC * CT], BF16)
    obuf = ctx.sbuf(tag + "_ob", [128, 4 * Dout], out_dtype)
    ps = [ctx.psum(tag + f"_ps{i}", [128, 512], F32) for i in range(6)]
    ctx.op("gpsimd", lambda g: g.dma_start(out=o1r[:, :], in_=onesrow[:, :]), inc=((ld, 16),))
    ctx.op("gpsimd", lambda g: g.dma_start(out=brow[:, :], in_=bias_row_hbm[:, :]), inc=((ld, 16),))
    ldw = ctx.val(ld)

    xsrc = act_fm.rearrange("(kc p) (c t) -> c p kc t", p=128, c=NCH)
    xdst = xsb.ap().rearrange("p (kc t) -> p kc t", kc=KC)
    odst = out_tm.rearrange("(c s p) d -> c p s d", c=NCH, s=4, p=128)
    osrc = obuf.ap().rearrange("p (s d) -> p s d", s=4)
    grp = 0
    ev0 = ctx.val(ev)
    for c in range(NCH):
        ctx.op("sync", lambda y, c=c: y.dma_start(out=xdst, in_=xsrc[c]),
               wait=((mm, ctx.val(mm)),), inc=((xnl, 16),))
        xv = ctx.val(xnl)
        for mt in range(4):
            for dh in range(ND2):
                bank = ps[grp % 6]
                w = [(xnl, xv), (ld, ldw)]
                if grp >= 6:
                    w.append((ev, ev0 + grp - 5))
                for kc in range(KC):
                    ctx.op("tensor", lambda t, bank=bank, kc=kc, mt=mt, dh=dh:
                           t.matmul(bank[:, :], xsb[:, kc * CT + mt * 128: kc * CT + mt * 128 + 128],
                                    wsb[:, kc * Dout + dh * 512: kc * Dout + (dh + 1) * 512],
                                    start=(kc == 0), stop=False),
                           wait=w if kc == 0 else ())
                ctx.op("tensor", lambda t, bank=bank, dh=dh:
                       t.matmul(bank[:, :], o1r[:, :], brow[:, dh * 512:(dh + 1) * 512],
                                start=False, stop=True), inc=((mm, 1),))
                ctx.op("vector", lambda v, bank=bank, mt=mt, dh=dh:
                       v.tensor_copy(obuf[:, mt * Dout + dh * 512: mt * Dout + (dh + 1) * 512],
                                     bank[:, :]),
                       wait=((mm, ctx.val(mm)),), inc=((ev, 1),))
                grp += 1
        ctx.op("sync", lambda y, c=c: y.dma_start(out=odst[c], in_=osrc),
               wait=((ev, ctx.val(ev)),), inc=((st, 16),))
    ctx.wait("sync", (st, ctx.val(st)))


def emit_mamba(ctx, tag, xn_fm, s0_tm, in_proj_w, out_proj_w, conv_w_c, conv_b_c,
               dtb_c, alog_c, d_rep_h, nw_c, tmask_h, cid_bf, cid_f32, nch=16):
    nc = ctx.nc
    NCH, CT, NH = nch, 256, 32
    (ld, xnl, tt, sx, ac, stt, st,
     sce, sct, ydm, stm, sxd, scu, spe, ypm, sel) = (tag + s_ for s_ in
        ("_ld", "_xn", "_tt", "_sx", "_ac", "_stt", "_st",
         "_sce", "_sct", "_ydm", "_stm", "_sxd", "_scu", "_spe", "_ypm", "_sel"))
    pool = PsumPool(ctx, tag, n=3, shape=(128, 512))
    tps = [ctx.psum(tag + f"_tp{i}", [128, 128], BF16) for i in range(2)]
    ppy = ctx.psum(tag + "_ppy", [128, 256], F32)
    pst = ctx.psum(tag + "_pst", [128, 128], F32)
    prms = ctx.psum(tag + "_prms", [128, 512], F32)

    ipT = ctx.sbuf(tag + "_ipT", [128, 8 * 4384], BF16)
    woT = ctx.sbuf(tag + "_woT", [128, 16 * 1024], BF16)
    stg = ctx.sbuf(tag + "_stg", [128, 2048], BF16)
    xnb = ctx.sbuf(tag + "_xnb", [128, 8 * CT], BF16)
    silz = ctx.sbuf(tag + "_silz", [128, 16 * CT], BF16)
    xbc = ctx.sbuf(tag + "_xbc", [128, 18 * 260], F32)
    cvt = ctx.sbuf(tag + "_cvt", [128, CT], F32)
    dtw = ctx.sbuf(tag + "_dtw", [32, 6 * CT], F32)
    acs_tm = ctx.sbuf(tag + "_acst", [128, 64], F32)
    dtt = ctx.sbuf(tag + "_dtt", [128, 64], F32)
    dtdet = ctx.sbuf(tag + "_dtdet", [128, 64], F32)
    x_tm = ctx.sbuf(tag + "_xtm", [128, 2 * 2048], BF16)
    xdts = ctx.sbuf(tag + "_xdts", [128, 4 * 64], BF16)
    xdds = ctx.sbuf(tag + "_xdds", [128, 4 * 64], BF16)
    carry = ctx.sbuf(tag + "_carry", [128, 2048], F32)
    carryb = ctx.sbuf(tag + "_carryb", [128, 2048], BF16)
    ybuf = ctx.sbuf(tag + "_ybuf", [128, 16 * CT], F32)
    ynb = ctx.sbuf(tag + "_ynb", [128, 16 * CT], BF16)
    ysq = ctx.sbuf(tag + "_ysq", [128, CT], F32)
    s0b = ctx.sbuf(tag + "_s0b", [128, 2 * 512], F32)
    cbtM = ctx.sbuf(tag + "_cbtM", [128, 512], F32)
    tmask = ctx.sbuf(tag + "_tmask", [128, 384], F32)
    acsr = ctx.sbuf(tag + "_acsr", [1, 512], F32)
    sc1 = ctx.sbuf(tag + "_sc1", [128, 2 * CT], F32)
    sc2 = ctx.sbuf(tag + "_sc2", [128, 2 * CT], F32)
    scob = ctx.sbuf(tag + "_scob", [128, 2 * CT], BF16)
    cpr = ctx.sbuf(tag + "_cpr", [128, 2 * CT], BF16)
    bcc = ctx.sbuf(tag + "_bcc", [128, 2 * CT], BF16)
    bmt = ctx.sbuf(tag + "_bmt", [128, 2 * 128], BF16)
    cwc = ctx.sbuf(tag + "_cwc", [128, 90], F32)
    smal = ctx.sbuf(tag + "_smal", [32, 8], F32)
    cdrow = ctx.sbuf(tag + "_cdrow", [1, 32], F32)
    cdbc = ctx.sbuf(tag + "_cdbc", [128, 32], F32)
    drep = ctx.sbuf(tag + "_drep", [128, 16], F32)
    nwc = ctx.sbuf(tag + "_nwc", [128, 16], F32)
    onef = ctx.sbuf(tag + "_onef", [1, 128], F32)
    onc = ctx.sbuf(tag + "_onc", [128, 1], F32)
    idb = ctx.sbuf(tag + "_idb", [128, 128], BF16)
    idf = ctx.sbuf(tag + "_idf", [128, 128], F32)
    eps1 = None

    ctx.op("sync", lambda g: g.dma_start(out=idb[:, :], in_=cid_bf[:, :]), inc=((ld, 16),))
    ctx.op("sync", lambda y: y.dma_start(out=idf[:, :], in_=cid_f32[:, :]), inc=((ld, 16),))
    ctx.op("sync", lambda y: y.dma_start(out=cwc[:, 0:72], in_=conv_w_c[:, :]), inc=((ld, 16),))
    ctx.op("sync", lambda y: y.dma_start(out=cwc[:, 72:90], in_=conv_b_c[:, :]), inc=((ld, 16),))
    ctx.op("sync", lambda y: y.dma_start(out=smal[:, 0:1], in_=dtb_c[:, :]), inc=((ld, 16),))
    ctx.op("sync", lambda y: y.dma_start(out=smal[:, 1:2], in_=alog_c[:, :]), inc=((ld, 16),))
    ctx.op("sync", lambda y: y.dma_start(out=drep[:, :], in_=d_rep_h[:, :]), inc=((ld, 16),))
    ctx.op("sync", lambda y: y.dma_start(out=nwc[:, :], in_=nw_c[:, :]), inc=((ld, 16),))
    ctx.op("sync", lambda y: y.dma_start(out=tmask[:, :], in_=tmask_h[:, :]), inc=((ld, 16),))
    ldv0 = ctx.val(ld)
    ctx.op("vector", lambda v: v.memset(onef[:, :], 1.0), inc=((sx, 1),))
    ctx.op("vector", lambda v: v.memset(onc[:, :], 1.0), inc=((sx, 1),))
    ctx.op("vector", lambda v: v.memset(carry[:, :], 0.0), inc=((sx, 1),))
    ctx.op("vector", lambda v: v.memset(dtw[:, 1280:1536], 0.0), inc=((sx, 1),))
    ctx.op("vector", lambda v: v.memset(
        xbc.ap().rearrange("p (cc w) -> p cc w", cc=18)[:, :, 0:4], 0.0), inc=((sx, 1),))
    ctx.op("scalar", lambda e: e.activation(smal[:, 2:3], smal[:, 1:2], AF.Exp),
           wait=((ld, ldv0),), inc=((ac, 1),))
    ctx.op("vector", lambda v: v.tensor_scalar(smal[:, 2:3], smal[:, 2:3], -1.0, None, ALU.mult),
           wait=((ac, ctx.val(ac)),), inc=((sx, 1),))
    sx0 = ctx.val(sx)

    lsems = (tag + "_lT", tag + "_tT", tag + "_eT")
    sI, vI = emit_load_w_T(ctx, tag + "ip", in_proj_w, 4384, 1024, ipT, idb,
                           stg, tps, after=[(ld, ldv0)], sems=lsems, nbuf=2)
    sW, vW = emit_load_w_T(ctx, tag + "wo", out_proj_w, 1024, 2048, woT, idb,
                           stg, tps, after=[(lsems[2], vI)], sems=lsems, nbuf=1)
    for kc in range(16):
        ctx.op("vector", lambda v, kc=kc: v.tensor_scalar(
            woT[:, kc * 1024:(kc + 1) * 1024], woT[:, kc * 1024:(kc + 1) * 1024],
            nwc[:, kc:kc + 1], None, ALU.mult),
            wait=((sW, vW), (ld, ldv0)) if kc == 0 else (), inc=((stt, 1),))
    wov = ctx.val(stt)

    xsrc = xn_fm.rearrange("(kc p) (c t) -> c p kc t", p=128, c=16)
    xdst = xnb.ap().rearrange("p (kc t) -> p kc t", kc=8)
    sdst = s0_tm.rearrange("(c m p) (dh d) -> c m dh p d", c=16, m=2, p=128, dh=2)

    g_sc = 0
    prev_ip = 0
    for c in range(NCH):
        ctx.op("sync", lambda y, c=c: y.dma_start(out=xdst, in_=xsrc[c]),
               wait=((pool.mm, prev_ip),), inc=((xnl, 16),))
        xv = ctx.val(xnl)
        for mc in range(35):
            w = min(128, 4384 - mc * 128)

            def mk(kc, mc=mc, w=w):
                return lambda t, bank: t.matmul(
                    bank[0:w, 0:CT], ipT[:, kc * 4384 + mc * 128: kc * 4384 + mc * 128 + w],
                    xnb[:, kc * CT:(kc + 1) * CT], start=(kc == 0), stop=(kc == 7))
            if mc < 16:
                evb = (lambda e, bank, mc=mc: e.activation(
                    silz[:, mc * CT:(mc + 1) * CT], bank[:, 0:CT], AF.Silu))
            elif mc < 34:
                cc = mc - 16
                evb = (lambda e, bank, cc=cc: e.copy(
                    xbc[:, cc * 260 + 4: cc * 260 + 260], bank[:, 0:CT]))
            else:
                evb = (lambda e, bank: e.copy(dtw[:, 0:CT], bank[0:32, 0:CT]))
            ew = []
            if c > 0 and mc == 0:
                ew = [(ydm, ctx.val(ydm)), (stm, ctx.val(stm)), (sxd, ctx.val(sxd)),
                      (stt, ctx.val(stt)), (sx, ctx.val(sx)), (tt, ctx.val(tt)),
                      (ac, ctx.val(ac)), (sct, ctx.val(sct)), (scu, ctx.val(scu)),
                      (spe, ctx.val(spe)), (sel, ctx.val(sel))]
            pool.group([mk(kc) for kc in range(8)], evb, evac_eng="scalar",
                       first_wait=[(xnl, xv), (sI, vI), (stt, wov)] if mc == 0 else [],
                       evac_wait=ew)
        ipev = ctx.val(pool.ev)
        prev_ip = ctx.val(pool.mm)

        for cc in range(18):
            w0 = [(pool.ev, ipev)] if cc == 0 else []
            ctx.op("vector", lambda v, cc=cc: v.tensor_scalar(
                cvt[:, :], xbc[:, cc * 260 + 1: cc * 260 + 1 + CT], cwc[:, cc:cc + 1],
                None, ALU.mult),
                wait=w0 + [(ac, ctx.val(ac))], inc=((stt, 1),))
            for k in (1, 2, 3):
                ctx.op("vector", lambda v, cc=cc, k=k: v.scalar_tensor_tensor(
                    cvt[:, :], xbc[:, cc * 260 + 1 + k: cc * 260 + 1 + k + CT],
                    cwc[:, k * 18 + cc: k * 18 + cc + 1], cvt[:, :], ALU.mult, ALU.add),
                    wait=((stt, ctx.val(stt)),), inc=((stt, 1),))
            ctx.op("vector", lambda v, cc=cc: v.tensor_copy(
                xbc[:, cc * 260 + 1: cc * 260 + 4], xbc[:, cc * 260 + 257: cc * 260 + 260]),
                wait=((stt, ctx.val(stt)),), inc=((stt, 1),))
            ctx.op("scalar", lambda e, cc=cc: e.activation(
                xbc[:, cc * 260 + 4: cc * 260 + 260], cvt[:, :], AF.Silu,
                bias=cwc[:, 72 + cc: 72 + cc + 1]),
                wait=((stt, ctx.val(stt)),), inc=((ac, 1),))
        convv = ctx.val(ac)

        ctx.op("vector", lambda v: v.tensor_scalar(
            dtw[:, 256:512], dtw[:, 0:CT], smal[:, 0:1], None, ALU.add),
            wait=((pool.ev, ipev), (sx, sx0)), inc=((stt, 1),))
        ctx.op("scalar", lambda e: e.activation(dtw[:, 512:768], dtw[:, 256:512], AF.Abs),
               wait=((stt, ctx.val(stt)),), inc=((ac, 1),))
        ctx.op("scalar", lambda e: e.activation(dtw[:, 512:768], dtw[:, 512:768], AF.Exp, scale=-1.0),
               wait=((ac, ctx.val(ac)),), inc=((ac, 1),))
        ctx.op("scalar", lambda e: e.activation(dtw[:, 512:768], dtw[:, 512:768], AF.Ln, bias=1.0),
               wait=((ac, ctx.val(ac)),), inc=((ac, 1),))
        ctx.op("vector", lambda v: v.tensor_scalar(dtw[:, 768:1024], dtw[:, 256:512], 0.0, None, ALU.max),
               wait=((ac, ctx.val(ac)),), inc=((stt, 1),))
        ctx.op("vector", lambda v: v.tensor_add(dtw[:, 768:1024], dtw[:, 768:1024], dtw[:, 512:768]),
               wait=((stt, ctx.val(stt)),), inc=((stt, 1),))
        ctx.op("vector", lambda v: v.tensor_scalar(dtw[:, 512:768], dtw[:, 768:1024],
                                                   smal[:, 2:3], None, ALU.mult),
               wait=((stt, ctx.val(stt)),), inc=((stt, 1),))
        ctx.op("vector", lambda v: v.tensor_tensor_scan(
            dtw[:, 1024:1280], dtw[:, 512:768], dtw[:, 1280:1536], 0.0, ALU.add, ALU.add),
            wait=((stt, ctx.val(stt)),), inc=((stt, 1),))
        ctx.op("scalar", lambda e: e.activation(dtw[:, 256:512], dtw[:, 1024:1280], AF.Exp,
                                                scale=-1.0, bias=dtw[:, 1279:1280]),
               wait=((stt, ctx.val(stt)),), inc=((ac, 1),))
        ctx.op("scalar", lambda e: e.activation(smal[:, 3:4], dtw[:, 1279:1280], AF.Exp),
               inc=((ac, 1),))
        ctx.op("vector", lambda v: v.tensor_mul(dtw[:, 256:512], dtw[:, 256:512], dtw[:, 768:1024]),
               wait=((ac, ctx.val(ac)),), inc=((stt, 1),))
        dtv = ctx.val(stt)
        acv = ctx.val(ac)

        for src0, dst in ((1024, acs_tm), (768, dtt), (256, dtdet)):
            for jc in range(2):
                pool.group([lambda t, bank, src0=src0, jc=jc: t.transpose(
                    bank[0:128, 0:32], dtw[:, src0 + jc * 128: src0 + jc * 128 + 128],
                    idf[0:32, 0:32])],
                    lambda e, bank, dst=dst, jc=jc: e.tensor_copy(
                        dst[:, jc * 32: jc * 32 + 32], bank[0:128, 0:32]),
                    first_wait=((stt, dtv), (ac, acv)))
        pool.group([lambda t, bank: t.transpose(bank[0:1, 0:32], smal[:, 3:4], idf[0:32, 0:32])],
                   lambda e, bank: e.tensor_copy(cdrow[:, :], bank[0:1, 0:32]),
                   first_wait=((ac, acv),))
        cdv = ctx.val(pool.ev)
        pool.group([lambda t, bank: t.matmul(bank[0:128, 0:32], onef[:, :], cdrow[:, :],
                                             start=True, stop=True)],
                   lambda e, bank: e.tensor_copy(cdbc[:, :], bank[0:128, 0:32]),
                   first_wait=((pool.ev, cdv),))

        ctx.op("scalar", lambda e: e.activation(bcc[:, 0:CT], xbc[:, 16 * 260 + 4: 16 * 260 + 260], AF.Copy),
               wait=((ac, convv), (ydm, ctx.val(ydm)), (ypm, ctx.val(ypm))), inc=((ac, 1),))
        ctx.op("scalar", lambda e: e.activation(bcc[:, CT:2 * CT], xbc[:, 17 * 260 + 4: 17 * 260 + 260], AF.Copy),
               inc=((ac, 1),))
        bcv = ctx.val(ac)
        for q in range(4):
            ctx.op("scalar", lambda e, q=q: e.activation(
                carryb[:, q * 512:(q + 1) * 512], carry[:, q * 512:(q + 1) * 512], AF.Copy),
                wait=((scu, ctx.val(scu)), (ypm, ctx.val(ypm))) if q == 0 else (),
                inc=((ac, 1),))
        cbv = ctx.val(ac)
        for jc in range(2):
            ctx.op("tensor", lambda t, jc=jc: t.transpose(
                tps[jc][:, :], bcc[:, jc * 128: jc * 128 + 128], idb[:, :]),
                wait=((ac, bcv), (sx, ctx.val(sx))), inc=((tt, 1),))
            ctx.op("vector", lambda v, jc=jc: v.tensor_copy(
                bmt[:, jc * 128:(jc + 1) * 128], tps[jc][:, :]),
                wait=((tt, ctx.val(tt)),), inc=((sx, 1),))
        for jc in range(2):
            pool.group([lambda t, bank, jc=jc: t.matmul(
                bank[:, 0:CT], bcc[:, jc * 128: jc * 128 + 128], bcc[:, CT:2 * CT],
                start=True, stop=True)],
                lambda e, bank, jc=jc: e.scalar_tensor_tensor(
                    cbtM[:, jc * CT:(jc + 1) * CT], bank[:, 0:CT], 1.0,
                    tmask[:, 128 - jc * 128: 384 - jc * 128], ALU.mult, ALU.mult),
                first_wait=((ac, bcv), (ld, ldv0)))
        for cc in range(16):
            for jc in range(2):
                pool.group([lambda t, bank, cc=cc, jc=jc: t.transpose(
                    bank[0:128, 0:128], xbc[:, cc * 260 + 4 + jc * 128: cc * 260 + 4 + jc * 128 + 128],
                    idf[:, :])],
                    lambda e, bank, cc=cc, jc=jc: e.tensor_copy(
                        x_tm[:, jc * 2048 + cc * 128: jc * 2048 + cc * 128 + 128],
                        bank[0:128, 0:128]),
                    first_wait=((ac, convv), (sxd, ctx.val(sxd))))
        xtv = ctx.val(pool.ev)

        NIT = 2 * NH
        b_ydm, b_stm = ctx.val(ydm), ctx.val(stm)
        b_sce, b_sct = ctx.val(sce), ctx.val(sct)
        b_sxd, b_ypm = ctx.val(sxd), ctx.val(ypm)
        extr_ev = {}
        rel_acsr = {}
        s1_ev = {}
        cpr_sel = {}

        def stage_head(h):
            hs = h % 2
            gw = [(pool.ev, xtv)]
            if h >= 2:
                gw.append((pool.mm, rel_acsr[h - 2]))
            for jc in range(2):
                pool.group([lambda t, bank, h=h, jc=jc: t.transpose(
                    bank[0:1, 0:128], acs_tm[:, jc * 32 + h: jc * 32 + h + 1], idf[:, :])],
                    lambda e, bank, hs=hs, jc=jc: e.tensor_copy(
                        acsr[0:1, hs * 256 + jc * 128: hs * 256 + jc * 128 + 128],
                        bank[0:1, 0:128]),
                    first_wait=gw if jc == 0 else ())
            extr_ev[h] = ctx.val(pool.ev)
            pool.group([lambda t, bank, hs=hs: t.matmul(
                bank[:, 0:CT], onef[:, :], acsr[0:1, hs * 256: hs * 256 + 256],
                start=True, stop=True)],
                lambda e, bank: e.activation(ysq[:, :], bank[:, 0:CT], AF.Exp),
                evac_eng="scalar",
                first_wait=((pool.ev, extr_ev[h]),), evac_wait=((sel, ctx.val(sel)),))
            ctx.op("vector", lambda v, hs=hs: v.tensor_mul(
                cpr[:, hs * CT:(hs + 1) * CT], bcc[:, CT:2 * CT], ysq[:, :]),
                wait=[(pool.ev, ctx.val(pool.ev)), (ac, cbv)] +
                     ([(ypm, b_ypm + h - 1)] if h >= 2 else
                      ([(ypm, b_ypm)] if b_ypm > 0 else [])),
                inc=((sel, 1),))
            cpr_sel[h] = ctx.val(sel)

        def stage0(g):
            h, jc = g // 2, g % 2
            sl = (g % 4) * 64
            wxd = [(pool.ev, xtv)]
            if g >= 4:
                wxd += [(ydm, b_ydm + g - 3), (stm, b_stm + g - 3)]
            ctx.op("vector", lambda v, h=h, jc=jc, sl=sl: v.tensor_scalar(
                xdts[:, sl: sl + 64],
                x_tm[:, jc * 2048 + h * 64: jc * 2048 + h * 64 + 64],
                dtt[:, jc * 32 + h: jc * 32 + h + 1], None, ALU.mult),
                wait=wxd, inc=((sxd, 1),))
            ctx.op("vector", lambda v, h=h, jc=jc, sl=sl: v.tensor_scalar(
                xdds[:, sl: sl + 64],
                x_tm[:, jc * 2048 + h * 64: jc * 2048 + h * 64 + 64],
                dtdet[:, jc * 32 + h: jc * 32 + h + 1], None, ALU.mult),
                inc=((sxd, 1),))

        def stage1(g):
            h, jc = g // 2, g % 2
            par = g % 2
            hs = h % 2
            ewv = [(sce, b_sce + g - 1)] if g >= 2 else []
            pool.group([lambda t, bank, hs=hs: t.matmul(
                bank[:, 0:CT], onef[:, :], acsr[0:1, hs * 256: hs * 256 + 256],
                start=True, stop=True)],
                lambda e, bank, h=h, jc=jc, par=par: e.tensor_scalar(
                    sc1[:, par * CT:(par + 1) * CT], bank[:, 0:CT],
                    acs_tm[:, jc * 32 + h: jc * 32 + h + 1], 0.0,
                    ALU.subtract, ALU.min),
                first_wait=((pool.ev, extr_ev[h]),), evac_wait=ewv)
            s1_ev[g] = ctx.val(pool.ev)
            if jc == 1:
                rel_acsr[h] = ctx.val(pool.mm)

        def stage2(g):
            par = g % 2
            wex = [(pool.ev, s1_ev[g])]
            if g >= 2:
                wex.append((sct, b_sct + g - 1))
            ctx.op("scalar", lambda e, par=par: e.activation(
                sc2[:, par * CT:(par + 1) * CT], sc1[:, par * CT:(par + 1) * CT], AF.Exp),
                wait=wex, inc=((sce, 1),))

        def stage3(g):
            jc, par = g % 2, g % 2
            wtt = [(sce, b_sce + g + 1)]
            if g >= 2:
                wtt.append((ydm, b_ydm + g - 1))
            ctx.op("vector", lambda v, jc=jc, par=par: v.tensor_mul(
                scob[:, par * CT:(par + 1) * CT], sc2[:, par * CT:(par + 1) * CT],
                cbtM[:, jc * CT:(jc + 1) * CT]),
                wait=wtt, inc=((sct, 1),))

        def stage4(g):
            h, jc = g // 2, g % 2
            par = g % 2
            hp = h % 2
            sl = (g % 4) * 64
            wyd = [(sct, b_sct + g + 1), (sxd, b_sxd + 2 * (g + 1))]
            if jc == 0 and hp == 0:
                wyd.append((spe, ctx.val(spe)))
            ctx.op("tensor", lambda t, jc=jc, par=par, hp=hp, sl=sl: t.matmul(
                ppy[hp * 64:(hp + 1) * 64, :], xdts[:, sl: sl + 64],
                scob[:, par * CT:(par + 1) * CT], start=(jc == 0), stop=False),
                wait=wyd, inc=((ydm, 1),))
            wst = []
            if jc == 0 and hp == 0:
                wst.append((scu, ctx.val(scu)))
            ctx.op("tensor", lambda t, jc=jc, hp=hp, sl=sl: t.matmul(
                pst[:, hp * 64:(hp + 1) * 64], bmt[:, jc * 128:(jc + 1) * 128],
                xdds[:, sl: sl + 64], start=(jc == 0), stop=(jc == 1)),
                wait=wst, inc=((stm, 1),))

        def post_head(h):
            hp = h % 2
            hs = h % 2
            pc = h // 2
            ctx.op("tensor", lambda t, h=h, hp=hp, hs=hs: t.matmul(
                ppy[hp * 64:(hp + 1) * 64, :], carryb[:, h * 64:(h + 1) * 64],
                cpr[:, hs * CT:(hs + 1) * CT], start=False, stop=True),
                wait=((sel, cpr_sel[h]),), inc=((ypm, 1),))
            if hp == 1:
                for hh in (h - 1, h):
                    ctx.op("vector", lambda v, hh=hh: v.scalar_tensor_tensor(
                        carry[:, hh * 64:(hh + 1) * 64], carry[:, hh * 64:(hh + 1) * 64],
                        cdbc[:, hh:hh + 1], pst[:, (hh % 2) * 64:((hh % 2) + 1) * 64],
                        ALU.mult, ALU.add),
                        wait=((stm, b_stm + 2 * (h + 1)),) if hh == h - 1 else (),
                        inc=((scu, 1),))
                ctx.op("vector", lambda v, pc=pc: v.scalar_tensor_tensor(
                    ybuf[:, pc * CT:(pc + 1) * CT], xbc[:, pc * 260 + 4: pc * 260 + 260],
                    drep[:, pc:pc + 1], ppy[:, :], ALU.mult, ALU.add),
                    wait=((ydm, b_ydm + 2 * (h + 1)), (ypm, b_ypm + h + 1)), inc=((spe, 1),))
                ctx.op("vector", lambda v, pc=pc: v.tensor_mul(
                    ybuf[:, pc * CT:(pc + 1) * CT], ybuf[:, pc * CT:(pc + 1) * CT],
                    silz[:, pc * CT:(pc + 1) * CT]),
                    wait=((spe, ctx.val(spe)),), inc=((stt, 1),))

        for pos in range(NIT + 4):
            if pos < NIT:
                if pos % 2 == 0:
                    stage_head(pos // 2)
                stage0(pos)
                stage1(pos)
            if 0 <= pos - 1 < NIT:
                stage2(pos - 1)
                stage3(pos - 1)
            if pos - 4 >= 0 and (pos - 4) % 2 == 0 and (pos - 4) // 2 < NH:
                post_head((pos - 4) // 2)
            if 0 <= pos - 2 < NIT:
                stage4(pos - 2)
        yv = ctx.val(stt)


        for pc in range(16):
            ctx.op("scalar", lambda e, pc=pc: e.activation(
                ysq[:, :], ybuf[:, pc * CT:(pc + 1) * CT], AF.Square),
                wait=[(stt, yv), (sel, ctx.val(sel))] + ([(tt, ctx.val(tt))] if pc >= 1 else []),
                inc=((ac, 1),))
            ctx.op("tensor", lambda t, pc=pc: t.matmul(
                prms[0:1, 0:CT], onc[:, :], ysq[:, :],
                start=(pc == 0), stop=(pc == 15)),
                wait=((ac, ctx.val(ac)),), inc=((tt, 1),))
        rstd = ysq[0:1, 0:CT]
        ctx.op("scalar", lambda e: e.activation(rstd, prms[0:1, 0:CT], AF.Ln,
                                                scale=1.0 / 2048, bias=ctx.ccol(1e-5)[0:1, :]),
               wait=((tt, ctx.val(tt)), ("cst", 16)), inc=((ac, 1),))
        ctx.op("scalar", lambda e: e.activation(rstd, rstd, AF.Exp, scale=-0.5),
               wait=((ac, ctx.val(ac)),), inc=((ac, 1),))
        ctx.op("tensor", lambda t: t.matmul(prms[0:128, 256:512], onef[:, :], rstd,
                                            start=True, stop=True),
               wait=((ac, ctx.val(ac)),), inc=((tt, 1),))
        rbv = ctx.val(tt)
        for pc in range(16):
            ctx.op("vector", lambda v, pc=pc: v.tensor_mul(
                ynb[:, pc * CT:(pc + 1) * CT], ybuf[:, pc * CT:(pc + 1) * CT],
                prms[0:128, 256:512][:, 0:CT]),
                wait=((tt, rbv),) if pc == 0 else (), inc=((sx, 1),))
        ynv = ctx.val(sx)

        for m in range(2):
            for dh in range(2):
                def mko(kc, m=m, dh=dh):
                    return lambda t, bank: t.matmul(
                        bank[:, :], ynb[:, kc * CT + m * 128: kc * CT + m * 128 + 128],
                        woT[:, kc * 1024 + dh * 512: kc * 1024 + (dh + 1) * 512],
                        start=(kc == 0), stop=(kc == 15))
                gi = 2 * m + dh
                stp = st + str(gi % 2)
                sb = s0b[:, (gi % 2) * 512: (gi % 2 + 1) * 512]
                pool.group([mko(kc) for kc in range(16)],
                           lambda e, bank, sb=sb: e.tensor_copy(sb, bank[:, :]),
                           first_wait=((sx, ynv),),
                           evac_wait=((stp, ctx.val(stp)),))
                ctx.op("sync", lambda y, c=c, m=m, dh=dh, sb=sb: y.dma_start(
                    out=sdst[c, m, dh], in_=sb),
                    wait=((pool.ev, ctx.val(pool.ev)),), inc=((stp, 16),))
    ctx.wait("sync", (st + "0", ctx.val(st + "0")), (st + "1", ctx.val(st + "1")))


def emit_attn(ctx, tag, q_fm, k_fm, v_tm, gnw_c_hbm, gnb_c_hbm, cid_f32,
              onesrow, mrow_hbm, o_fm):
    nc = ctx.nc
    NH, DH = 16, 64
    NKJ = L // 128
    NQ = L // 512
    ld, pvm, trs, sx, st = (tag + s for s in ("_ld", "_pvm", "_trs", "_sx", "_st"))
    pool = PsumPool(ctx, tag, n=6)
    ppv = ctx.psum(tag + "_ppv", [128, 512], F32)
    pbc = ctx.psum(tag + "_pbc", [128, 512], F32)

    NPT = 4
    PVLAG = 3
    qh = ctx.sbuf(tag + "_qh", [128, L], BF16)
    kh = ctx.sbuf(tag + "_kh", [128, L], BF16)
    vaug = ctx.sbuf(tag + "_vaug", [128, NKJ * 72], BF16)
    ptile = [ctx.sbuf(tag + f"_pt{i}", [128, 512], BF16) for i in range(NPT)]
    sqb = ctx.sbuf(tag + "_sqb", [64, 2 * L], BF16)
    on64b = ctx.sbuf(tag + "_o64b", [64, 1], BF16)
    onesf = ctx.sbuf(tag + "_onsf", [1, 128], F32)
    nbs = ctx.sbuf(tag + "_nbs", [1, 8], F32)
    nbc = ctx.sbuf(tag + "_nbc", [128, 1], F32)
    p4tmp = ctx.sbuf(tag + "_p4t", [128, 16], F32)
    onrm = ctx.sbuf(tag + "_onrm", [64, L], F32)
    ob64 = ctx.sbuf(tag + "_ob64", [64, 512], F32)
    obh = ctx.sbuf(tag + "_obh", [64, L], BF16)
    rr = ctx.sbuf(tag + "_rr", [1, 512], F32)
    sc8 = ctx.sbuf(tag + "_sc8", [64, 18], F32)
    msr = ctx.sbuf(tag + "_msr", [64, 2], F32)
    gcol = ctx.sbuf(tag + "_gcol", [64, 2], F32)
    tot = ctx.sbuf(tag + "_tot", [1, 8], F32)
    idf = ctx.sbuf(tag + "_idf", [128, 128], F32)
    on64 = ctx.sbuf(tag + "_on64", [64, 1], F32)
    on1r = ctx.sbuf(tag + "_on1r", [1, 64], F32)
    gnw = ctx.sbuf(tag + "_gnw", [64, 16], F32)
    gnb = ctx.sbuf(tag + "_gnb", [64, 16], F32)
    o1b = ctx.sbuf(tag + "_o1b", [1, 128], BF16)

    ctx.op("sync", lambda y: y.dma_start(out=idf[:, :], in_=cid_f32[:, :]), inc=((ld, 16),))
    ctx.op("sync", lambda y: y.dma_start(out=gnw[:, :], in_=gnw_c_hbm[:, :]), inc=((ld, 16),))
    ctx.op("sync", lambda y: y.dma_start(out=gnb[:, :], in_=gnb_c_hbm[:, :]), inc=((ld, 16),))
    ctx.op("vector", lambda v: v.memset(on64[:, :], 1.0), inc=((sx, 1),))
    ctx.op("vector", lambda v: v.memset(on1r[:, :], 1.0), inc=((sx, 1),))
    ctx.op("vector", lambda v: v.memset(on64b[:, :], 1.0), inc=((sx, 1),))
    ctx.op("vector", lambda v: v.memset(onesf[:, :], 1.0), inc=((sx, 1),))
    ctx.op("sync", lambda y: y.dma_start(out=o1b[:, :], in_=onesrow[:, :]), inc=((ld, 16),))
    ctx.op("vector", lambda v: v.memset(
        vaug.ap().rearrange("p (kj w) -> p kj w", kj=NKJ)[:, :, 64:65], 1.0),
        inc=((sx, 1),))
    ctx.op("vector", lambda v: v.memset(
        vaug.ap().rearrange("p (kj w) -> p kj w", kj=NKJ)[:, :, 65:66], 0.0),
        inc=((sx, 1),))
    ctx.op("vector", lambda v: v.memset(qh[64:128, :], 0.0), inc=((sx, 1),))
    ctx.op("vector", lambda v: v.memset(kh[64:128, :], 0.0), inc=((sx, 1),))
    ldc = ctx.val(ld)

    vsrc = v_tm.rearrange("(kj p) d -> p kj d", p=128)
    vdst = vaug.ap().rearrange("p (kj w) -> p kj w", kj=NKJ)
    eps = ctx.ccol(1e-5)

    for h in range(NH):
        hb = h * DH
        gw = [(pool.mm, ctx.val(pool.mm)), (pvm, ctx.val(pvm))]
        ctx.op("sync", lambda y, hb=hb: y.dma_start(out=qh[0:64, :], in_=q_fm[hb:hb + 64, :]),
               wait=gw, inc=((ld, 16),))
        ctx.op("sync", lambda y, hb=hb: y.dma_start(out=kh[0:64, :], in_=k_fm[hb:hb + 64, :]),
               inc=((ld, 16),))
        ctx.op("sync", lambda y, hb=hb: y.dma_start(out=vdst[:, :, 0:64], in_=vsrc[:, :, hb:hb + 64]),
               wait=((sx, 1),), inc=((ld, 16),))
        ldh = ctx.val(ld)

        for ch in range(8):
            ctx.op("vector", lambda v, ch=ch: v.tensor_mul(
                sqb[:, ch * 512:(ch + 1) * 512], qh[0:64, ch * 512:(ch + 1) * 512],
                qh[0:64, ch * 512:(ch + 1) * 512]),
                wait=((ld, ldh),) if ch == 0 else (), inc=((sx, 1),))
            ctx.op("vector", lambda v, ch=ch: v.tensor_mul(
                sqb[:, L + ch * 512: L + (ch + 1) * 512], kh[0:64, ch * 512:(ch + 1) * 512],
                kh[0:64, ch * 512:(ch + 1) * 512]),
                inc=((sx, 1),))
        sxv = ctx.val(sx)
        for ch in range(16):
            def mkn(ch=ch):
                return lambda t, bank: t.matmul(
                    bank[0:1, 0:512], on64b[:, :], sqb[:, ch * 512:(ch + 1) * 512],
                    start=True, stop=True)
            pool.group([mkn()], lambda e, bank, ch=ch: e.reduce_max(
                p4tmp[0:1, ch:ch + 1], bank[0:1, 0:512], axis=AX.X),
                first_wait=((sx, sxv),))
        ctx.op("vector", lambda v: v.reduce_max(nbs[:, 0:1], p4tmp[0:1, 0:8], axis=AX.X),
               wait=((pool.ev, ctx.val(pool.ev)),), inc=((sx, 1),))
        ctx.op("vector", lambda v: v.reduce_max(nbs[:, 1:2], p4tmp[0:1, 8:16], axis=AX.X),
               inc=((sx, 1),))
        ctx.op("vector", lambda v: v.tensor_mul(nbs[:, 2:3], nbs[:, 0:1], nbs[:, 1:2]),
               wait=((sx, ctx.val(sx)),), inc=((sx, 1),))
        ctx.op("scalar", lambda e: e.activation(nbs[:, 3:4], nbs[:, 2:3], AF.Sqrt),
               wait=((sx, ctx.val(sx)),), inc=((sx, 1),))
        ctx.op("vector", lambda v: v.tensor_scalar(nbs[:, 4:5], nbs[:, 3:4], -1.0, None, ALU.mult),
               wait=((sx, ctx.val(sx)),), inc=((sx, 1),))
        ctx.op("tensor", lambda t: t.matmul(pbc[0:128, 0:1], onesf[:, :], nbs[:, 4:5],
                                            start=True, stop=True),
               wait=((sx, ctx.val(sx)),), inc=((trs, 1),))
        ctx.op("vector", lambda v: v.tensor_copy(nbc[:, :], pbc[0:128, 0:1]),
               wait=((trs, ctx.val(trs)), (pool.ev, ctx.val(pool.ev))), inc=((sx, 1),))
        nmv = ctx.val(sx)

        pvw0 = ctx.val(sx)
        for Q in range(NQ):
            q0 = Q * 512
            exp_ev = {}
            for kj in range(NKJ + PVLAG):
                par = kj % NPT
                if kj < NKJ:
                    def mkS(kj=kj, q0=q0):
                        return lambda t, bank: t.matmul(
                            bank[:, :], kh[:, kj * 128:(kj + 1) * 128],
                            qh[:, q0:q0 + 512], start=True, stop=True)
                    ew = [(pvm, ctx.val(pvm) - (NPT - 1 - PVLAG))] if kj >= NPT else []
                    pool.group([mkS()],
                               lambda e, bank, par=par: e.activation(
                                   ptile[par][:, :], bank[:, :], AF.Exp,
                                   bias=nbc[:, 0:1]),
                               evac_eng="scalar",
                               first_wait=((sx, nmv),), evac_wait=ew)
                    exp_ev[kj] = ctx.val(pool.ev)
                if kj >= PVLAG:
                    kp = kj - PVLAG
                    ctx.op("tensor", lambda t, kp=kp: t.matmul(
                        ppv[0:66, :], vaug[:, kp * 72: kp * 72 + 66], ptile[kp % NPT][:, :],
                        start=(kp == 0), stop=(kp == NKJ - 1)),
                        wait=[(pool.ev, exp_ev[kp])] + ([(sx, pvw0)] if kp == 0 else []),
                        inc=((pvm, 1),))
            ctx.op("vector", lambda v: v.tensor_copy(ob64[:, :], ppv[0:64, :]),
                   wait=((pvm, ctx.val(pvm)),), inc=((sx, 1),))
            ctx.op("scalar", lambda e: e.activation(rr[:, :], ppv[64:65, :], AF.Ln),
                   wait=((pvm, ctx.val(pvm)),), inc=((sx, 1),))
            ctx.op("scalar", lambda e: e.activation(rr[:, :], rr[:, :], AF.Exp, scale=-1.0),
                   wait=((sx, ctx.val(sx)),), inc=((sx, 1),))
            pvw0 = ctx.val(sx)
            ctx.op("tensor", lambda t: t.matmul(pbc[0:64, :], on1r[:, :], rr[:, :],
                                                start=True, stop=True),
                   wait=((sx, ctx.val(sx)),), inc=((trs, 1),))
            ctx.op("vector", lambda v, q0=q0: v.tensor_mul(onrm[:, q0:q0 + 512], ob64[:, :], pbc[0:64, :]),
                   wait=((trs, ctx.val(trs)),), inc=((sx, 1),))
            ctx.op("scalar", lambda e, Q=Q, q0=q0: e.activation(
                ob64[:, :], onrm[:, q0:q0 + 512], AF.Identity, accum_out=sc8[:, Q:Q + 1]),
                wait=((sx, ctx.val(sx)),), inc=((sx, 1),))
            ctx.op("scalar", lambda e, Q=Q, q0=q0: e.activation(
                ob64[:, :], onrm[:, q0:q0 + 512], AF.Square, accum_out=sc8[:, 8 + Q:9 + Q]),
                wait=((sx, ctx.val(sx)),), inc=((sx, 1),))

        ctx.op("vector", lambda v: v.reduce_sum(sc8[:, 16:17], sc8[:, 0:8], axis=AX.X),
               wait=((sx, ctx.val(sx)),), inc=((sx, 1),))
        ctx.op("vector", lambda v: v.reduce_sum(sc8[:, 17:18], sc8[:, 8:16], axis=AX.X),
               inc=((sx, 1),))
        ctx.op("tensor", lambda t: t.matmul(pbc[0:1, 0:2], on64[:, :], sc8[:, 16:18],
                                            start=True, stop=True),
               wait=((sx, ctx.val(sx)),), inc=((trs, 1),))
        ctx.op("vector", lambda v: v.tensor_scalar(tot[:, 0:2], pbc[0:1, 0:2],
                                                   1.0 / (L * 64), None, ALU.mult),
               wait=((trs, ctx.val(trs)),), inc=((sx, 1),))
        ctx.op("vector", lambda v: v.tensor_mul(tot[:, 2:3], tot[:, 0:1], tot[:, 0:1]),
               wait=((sx, ctx.val(sx)),), inc=((sx, 1),))
        ctx.op("vector", lambda v: v.tensor_sub(tot[:, 3:4], tot[:, 1:2], tot[:, 2:3]),
               wait=((sx, ctx.val(sx)),), inc=((sx, 1),))
        ctx.op("scalar", lambda e: e.activation(tot[:, 4:5], tot[:, 3:4], AF.Ln, bias=eps[0:1, :]),
               wait=((sx, ctx.val(sx)), ("cst", 16)), inc=((sx, 1),))
        ctx.op("scalar", lambda e: e.activation(tot[:, 4:5], tot[:, 4:5], AF.Exp, scale=-0.5),
               wait=((sx, ctx.val(sx)),), inc=((sx, 1),))
        ctx.op("vector", lambda v: v.tensor_copy(tot[:, 5:6], tot[:, 0:1]),
               wait=((sx, ctx.val(sx)),), inc=((sx, 1),))
        ctx.op("vector", lambda v: v.tensor_copy(tot[:, 6:7], tot[:, 4:5]),
               wait=((sx, ctx.val(sx)),), inc=((sx, 1),))
        ctx.op("tensor", lambda t: t.matmul(pbc[0:64, 0:2], on1r[:, :], tot[:, 5:7],
                                            start=True, stop=True),
               wait=((sx, ctx.val(sx)),), inc=((trs, 1),))
        ctx.op("vector", lambda v: v.tensor_copy(msr[:, :], pbc[0:64, 0:2]),
               wait=((trs, ctx.val(trs)),), inc=((sx, 1),))
        ctx.op("vector", lambda v, h=h: v.tensor_mul(gcol[:, 0:1], gnw[:, h:h + 1], msr[:, 1:2]),
               wait=((sx, ctx.val(sx)),), inc=((sx, 1),))
        ctx.op("vector", lambda v: v.tensor_mul(gcol[:, 1:2], gcol[:, 0:1], msr[:, 0:1]),
               wait=((sx, ctx.val(sx)),), inc=((sx, 1),))
        ctx.op("vector", lambda v, h=h: v.tensor_sub(gcol[:, 1:2], gnb[:, h:h + 1], gcol[:, 1:2]),
               wait=((sx, ctx.val(sx)),), inc=((sx, 1),))
        gv = ctx.val(sx)
        for Q in range(NQ):
            ctx.op("vector", lambda v, Q=Q: v.tensor_scalar(
                obh[:, Q * 512:(Q + 1) * 512], onrm[:, Q * 512:(Q + 1) * 512],
                gcol[:, 0:1], gcol[:, 1:2], ALU.mult, ALU.add),
                wait=[(sx, gv)] + ([(st, ctx.val(st))] if Q == 0 else []), inc=((sx, 1),))
        ctx.op("sync", lambda y, hb=hb: y.dma_start(out=o_fm[hb:hb + 64, :], in_=obh[:, :]),
               wait=((sx, ctx.val(sx)),), inc=((st, 16),))
    ctx.wait("sync", (st, ctx.val(st)))


def emit_load_w_T(ctx, tag, w2d, Dout, Din, dest, idb, stg, tps, after=(),
                  sems=None, nbuf=2):
    KC = Din // 128
    ldt, trt, evt = sems if sems else (tag + "_lT", tag + "_tT", tag + "_eT")
    noc = (Dout + 127) // 128
    for oc in range(noc):
        rows = min(128, Dout - oc * 128)
        par = oc % nbuf
        ldtp = ldt + str(par)
        gw = list(after) if oc == 0 else []
        tv = ctx.val(trt)
        if oc >= nbuf:
            gw.append((trt, tv - (nbuf - 1) * KC))
        ctx.op("gpsimd", lambda g, oc=oc, rows=rows, par=par:
               g.dma_start(out=stg[0:rows, par * Din:par * Din + Din],
                           in_=w2d[oc * 128: oc * 128 + rows, :]),
               wait=gw, inc=((ldtp, 16),))
        ldv = ctx.val(ldtp)
        for kc in range(KC):
            bank = tps[(oc * KC + kc) % 2]
            bw = [(ldtp, ldv)] if kc == 0 else []
            ev = ctx.val(evt)
            if ev >= 2:
                bw.append((evt, ev - 1))
            ctx.op("tensor", lambda t, oc=oc, kc=kc, rows=rows, par=par, bank=bank:
                   t.transpose(bank[0:128, 0:rows],
                               stg[0:rows, par * Din + kc * 128: par * Din + kc * 128 + 128],
                               idb[0:rows, 0:rows]),
                   wait=bw, inc=((trt, 1),))
            ctx.op("vector", lambda v, oc=oc, kc=kc, rows=rows, bank=bank:
                   v.tensor_copy(dest[:, kc * Dout + oc * 128: kc * Dout + oc * 128 + rows],
                                 bank[0:128, 0:rows]),
                   wait=((trt, ctx.val(trt)),), inc=((evt, 1),))
    return evt, ctx.val(evt)


def emit_l0moe(ctx, tag, xn_fm, u_tm, s0_tm, x1_tm, g0acc,
               gate_w, b1_c, b2_r, w1, w2, cid_bf, onesrow, wload_wait=()):
    nc = ctx.nc
    NE, F = 8, 4096
    CT = 512
    NCH = L // CT
    ld, wld, xnl, rt, tt, st = (tag + s for s in ("_ld", "_wld", "_xn", "_rt", "_tt", "_st"))
    pool = PsumPool(ctx, tag, n=6)
    tps = [ctx.psum(tag + f"_tp{i}", [128, 128], BF16) for i in range(2)]

    w1T = ctx.sbuf(tag + "_w1T", [128, 8 * F], BF16)
    w2T = ctx.sbuf(tag + "_w2T", [128, 32 * D], BF16)
    stg2 = ctx.sbuf(tag + "_stg2", [128, 2 * F], BF16)
    stg = stg2
    xnb = ctx.sbuf(tag + "_xnb", [128, 8 * CT], BF16)
    actar = ctx.arena(tag + "_actar", 32768)
    act = carve(actar, 0, 32 * CT, BF16)
    acc = ctx.sbuf(tag + "_acc", [128, 4 * D], F32)
    wall = ctx.sbuf(tag + "_wall", [128, 32 * NE], F32)
    lgb = ctx.sbuf(tag + "_lgb", [128, 2 * NE], F32)
    pt8 = ctx.sbuf(tag + "_pt8", [128, NE], F32)
    rc = ctx.sbuf(tag + "_rc", [128, 8], F32)
    grhs = ctx.sbuf(tag + "_grhs", [128, 8 * NE], BF16)
    gstg = carve(actar, 0, D, BF16, parts=8)
    b1cv = ctx.sbuf(tag + "_b1c", [128, 32], F32)
    b2v = ctx.sbuf(tag + "_b2v", [1, D], BF16)
    o1r = ctx.sbuf(tag + "_o1r", [1, 128], BF16)
    idb = ctx.sbuf(tag + "_idb", [128, 128], BF16)
    epi = [carve(actar, i * 16384, 4 * D, F32) for i in range(2)]

    ctx.op("gpsimd", lambda g: g.dma_start(out=o1r[:, :], in_=onesrow[:, :]), inc=((ld, 16),))
    ctx.op("gpsimd", lambda g: g.dma_start(out=idb[:, :], in_=cid_bf[:, :]), inc=((ld, 16),))
    ctx.op("gpsimd", lambda g: g.dma_start(out=gstg[:, :], in_=gate_w[:, :]), inc=((ld, 16),))
    ldv0 = ctx.val(ld)
    for kc in range(8):
        ctx.op("tensor", lambda t, kc=kc: t.transpose(
            tps[kc % 2][0:128, 0:8], gstg[:, kc * 128:(kc + 1) * 128], idb[0:8, 0:8]),
            wait=[(ld, ldv0)] + ([(rt, ctx.val(rt) - 1)] if kc >= 2 else []),
            inc=((tt, 1),))
        ctx.op("vector", lambda v, kc=kc: v.tensor_copy(
            grhs[:, kc * NE:(kc + 1) * NE], tps[kc % 2][0:128, 0:8]),
            wait=((tt, ctx.val(tt)),), inc=((rt, 1),))

    xsrc = xn_fm.rearrange("(kc p) (c t) -> c p kc t", p=128, c=NCH)
    xdst = xnb.ap().rearrange("p (kc t) -> p kc t", kc=8)
    accsrc3 = acc.ap().rearrange("p (s d) -> p s d", s=4)
    gdst = g0acc.rearrange("(c s p) d -> c p s d", c=NCH, s=4, p=128)

    prev_mmA = 0
    for c in range(NCH):
        ctx.op("sync", lambda y, c=c: y.dma_start(out=xdst, in_=xsrc[c]),
               wait=((pool.mm, prev_mmA),), inc=((xnl, 16),))
        xv = ctx.val(xnl)
        for mt in range(4):
            chunk = c * 4 + mt
            wcol = wall[:, chunk * NE:(chunk + 1) * NE]

            def mk(kc, mt=mt):
                return lambda t, bank: t.matmul(
                    bank[:, 0:NE], xnb[:, kc * CT + mt * 128: kc * CT + mt * 128 + 128],
                    grhs[:, kc * NE:(kc + 1) * NE], start=(kc == 0), stop=(kc == 7))
            pool.group([mk(kc) for kc in range(8)],
                       lambda e, bank: e.tensor_copy(lgb[:, 0:NE], bank[:, 0:NE]),
                       first_wait=((xnl, xv), (rt, ctx.val(rt))),
                       evac_wait=((rt, ctx.val(rt)),))
            ctx.op("vector", lambda v: v.reduce_max(rc[:, 1:2], lgb[:, 0:NE], axis=AX.X),
                   wait=((pool.ev, ctx.val(pool.ev)),), inc=((rt, 1),))
            ctx.op("vector", lambda v: v.tensor_scalar(rc[:, 0:1], rc[:, 1:2], -1.0, None, ALU.mult),
                   wait=((rt, ctx.val(rt)),), inc=((rt, 1),))
            ctx.op("vector", lambda v: v.scalar_tensor_tensor(
                lgb[:, NE:2 * NE], lgb[:, 0:NE], rc[:, 1:2], lgb[:, 0:NE], ALU.is_equal, ALU.bypass),
                wait=((rt, ctx.val(rt)),), inc=((rt, 1),))
            ctx.op("vector", lambda v: v.scalar_tensor_tensor(
                lgb[:, NE:2 * NE], lgb[:, NE:2 * NE], -1e30, lgb[:, 0:NE], ALU.mult, ALU.add),
                wait=((rt, ctx.val(rt)),), inc=((rt, 1),))
            ctx.op("vector", lambda v: v.reduce_max(rc[:, 2:3], lgb[:, NE:2 * NE], axis=AX.X),
                   wait=((rt, ctx.val(rt)),), inc=((rt, 1),))
            ctx.op("scalar", lambda e: e.activation(pt8[:, :], lgb[:, 0:NE], AF.Exp,
                                                    bias=rc[:, 0:1], accum_out=rc[:, 3:4]),
                   wait=((rt, ctx.val(rt)),), inc=((rt, 1),))
            ctx.op("vector", lambda v: v.reciprocal(rc[:, 4:5], rc[:, 3:4]),
                   wait=((rt, ctx.val(rt)),), inc=((rt, 1),))
            ctx.op("vector", lambda v: v.tensor_scalar(
                lgb[:, NE:2 * NE], lgb[:, 0:NE], rc[:, 2:3], None, ALU.is_ge),
                wait=((rt, ctx.val(rt)),), inc=((rt, 1),))
            ctx.op("vector", lambda v, wcol=wcol: v.scalar_tensor_tensor(
                wcol, pt8[:, :], rc[:, 4:5], lgb[:, NE:2 * NE], ALU.mult, ALU.mult),
                wait=((rt, ctx.val(rt)),), inc=((rt, 1),))
        prev_mmA = ctx.val(pool.mm)

    for e_ in range(NE):
        mmv = ctx.val(pool.mm)
        w1guard = mmv if e_ == 0 else prev_mmA
        lsems = (tag + "_lT", tag + "_tT", tag + "_eT")
        s1, v1 = emit_load_w_T(ctx, tag + f"w1{e_}", w1[:, e_, :], F, D, w1T, idb,
                               stg, tps,
                               after=[(pool.mm, w1guard), (tt, ctx.val(tt))] + list(wload_wait),
                               sems=lsems, nbuf=2)
        s2, v2 = emit_load_w_T(ctx, tag + f"w2{e_}", w2[:, e_, :], D, F, w2T, idb,
                               stg2, tps,
                               after=[(pool.mm, mmv), (lsems[1], ctx.val(lsems[1]))],
                               sems=lsems)
        ctx.op("sync", lambda y, e_=e_: y.dma_start(out=b1cv[:, :], in_=b1_c[e_]),
               inc=((wld, 16),))
        ctx.op("gpsimd", lambda g, e_=e_: g.dma_start(out=b2v[:, :], in_=b2_r[e_]),
               inc=((tag + "_wldg", 16),))
        wldv = ctx.val(wld)
        wldgv = ctx.val(tag + "_wldg")
        for c in range(NCH):
            if e_ > 0:
                ctx.op("sync", lambda y, c=c: y.dma_start(out=accsrc3, in_=gdst[c]),
                       wait=((st, ctx.val(st)),), inc=((xnl, 16),))
            ctx.op("sync", lambda y, c=c: y.dma_start(out=xdst, in_=xsrc[c]),
                   wait=((pool.mm, prev_mmA),) + (((st, ctx.val(st)),) if e_ == 0 else ()),
                   inc=((xnl, 16),))
            xv = ctx.val(xnl)
            ttv0 = ctx.val(tt)
            for fc in range(32):
                def mk(kc, fc=fc):
                    return lambda t, bank: t.matmul(
                        bank[:, :], w1T[:, kc * F + fc * 128: kc * F + fc * 128 + 128],
                        xnb[:, kc * CT:(kc + 1) * CT], start=(kc == 0), stop=(kc == 7))
                pool.group([mk(kc) for kc in range(8)],
                           lambda e, bank, fc=fc: e.activation(
                               act[:, fc * CT:(fc + 1) * CT], bank[:, :],
                               AF.Gelu_apprx_tanh, bias=b1cv[:, fc:fc + 1]),
                           evac_eng="scalar",
                           first_wait=[(xnl, xv), (s1, v1), (wld, wldv),
                                       (tag + "_wldg", wldgv)] if fc == 0
                           else [],
                           evac_wait=((tt, ttv0),) if fc == 0 else ())
            prev_mmA = ctx.val(pool.mm)
            evB = ctx.val(pool.ev)
            for mt in range(4):
                for dh in range(2):
                    def mkb(kc, mt=mt, dh=dh):
                        return lambda t, bank: t.matmul(
                            bank[:, :], act[:, kc * CT + mt * 128: kc * CT + mt * 128 + 128],
                            w2T[:, kc * D + dh * 512: kc * D + (dh + 1) * 512],
                            start=(kc == 0), stop=False)
                    mms = [mkb(kc) for kc in range(32)]
                    mms.append(lambda t, bank, dh=dh: t.matmul(
                        bank[:, :], o1r[:, :], b2v[:, dh * 512:(dh + 1) * 512],
                        start=False, stop=True))
                    dst = acc[:, mt * D + dh * 512: mt * D + (dh + 1) * 512]
                    wcol = wall[:, (c * 4 + mt) * NE + e_: (c * 4 + mt) * NE + e_ + 1]
                    if e_ == 0:
                        pool.group(mms, lambda e, bank, dst=dst, wcol=wcol:
                                   e.tensor_scalar(dst, bank[:, :], wcol, None, ALU.mult),
                                   first_wait=((pool.ev, evB), (st, ctx.val(st))),
                                   evac_wait=((tt, ctx.val(tt)),))
                    else:
                        pool.group(mms, lambda e, bank, dst=dst, wcol=wcol:
                                   e.scalar_tensor_tensor(dst, bank[:, :], wcol, dst,
                                                          ALU.mult, ALU.add),
                                   first_wait=((pool.ev, evB),),
                                   evac_wait=((tt, ctx.val(tt)),))
            ctx.op("tensor", None, inc=((tt, 1),))
            ctx.op("sync", lambda y, c=c: y.dma_start(out=gdst[c], in_=accsrc3),
                   wait=((pool.ev, ctx.val(pool.ev)),), inc=((st, 16),))

    usrc = u_tm.rearrange("(c s p) d -> c p s d", c=NCH, s=4, p=128)
    ssrc = s0_tm.rearrange("(c s p) d -> c p s d", c=NCH, s=4, p=128)
    xdst1 = x1_tm.rearrange("(c s p) d -> c p s d", c=NCH, s=4, p=128)
    epi3 = [v.rearrange("p (s d) -> p s d", s=4) for v in epi]
    mmv = ctx.val(pool.mm)
    for c in range(NCH):
        ctx.op("sync", lambda y, c=c: y.dma_start(out=accsrc3, in_=gdst[c]),
               wait=((pool.ev, ctx.val(pool.ev)), (pool.mm, mmv), (rt, ctx.val(rt))),
               inc=((xnl, 16),))
        ctx.op("sync", lambda y, c=c: y.dma_start(out=epi3[0], in_=usrc[c]),
               inc=((xnl, 16),))
        ctx.op("sync", lambda y, c=c: y.dma_start(out=epi3[1], in_=ssrc[c]),
               inc=((xnl, 16),))
        xnv = ctx.val(xnl)
        ctx.op("scalar", lambda e: e.activation(acc[:, :], acc[:, :], AF.Silu),
               wait=((xnl, xnv),), inc=((rt, 1),))
        ctx.op("vector", lambda v: v.tensor_mul(epi[1], epi[1], acc[:, :]),
               wait=((rt, ctx.val(rt)),), inc=((rt, 1),))
        ctx.op("vector", lambda v: v.tensor_add(epi[1], epi[1], epi[0]),
               wait=((rt, ctx.val(rt)),), inc=((rt, 1),))
        ctx.op("sync", lambda y, c=c: y.dma_start(out=xdst1[c], in_=epi3[1]),
               wait=((rt, ctx.val(rt)),), inc=((st, 16),))
    ctx.wait("sync", (st, ctx.val(st)))


def emit_l1ffn(ctx, tag, xn_fm, x1_tm, s1_tm, out_tm, g1acc,
               ws1, bs1_c, bs2_r, ws2, we1, be1_c, we2, be2_r, wr, br_r,
               onesrow, wload_wait=(), sparse=None, cid_f32=None, cid_bf=None):
    nc = ctx.nc
    NSL = 8
    CT = 512
    HS = 2048
    HR = 3072
    ld, wld, xnl, al, tt, rt, st = (tag + s for s in
                                    ("_ld", "_wld", "_xn", "_al", "_tt", "_rt", "_st"))
    pool = PsumPool(ctx, tag, n=(6 if sparse is not None else 8))

    war = ctx.arena(tag + "_war", 147456)
    ws1v = carve(war, 0, 32768, BF16)
    ws2v = carve(war, 65536, 16384, BF16)
    we1v = carve(war, 0, 49152, BF16)
    we2v = carve(war, 98304, 24576, BF16)
    epi = [carve(war, i * 16384, 4096, F32) for i in range(4)]
    act = ctx.sbuf(tag + "_act", [128, 24 * CT], BF16)
    if sparse is None:
        acc = ctx.sbuf(tag + "_acc", [128, 4 * D], F32)
        xnb = ctx.sbuf(tag + "_xnb", [128, 8 * CT], BF16)
    else:
        acc = carve(war, 98304, 4 * D, F32)
        xnb = carve(war, 114688, 8 * CT, BF16)
    a_t = ctx.sbuf(tag + "_at", [128, CT + 64], F32)
    g_t = ctx.sbuf(tag + "_gt", [128, CT], F32)
    wall = ctx.sbuf(tag + "_wall", [128, 32 * 12], F32)
    lgb = ctx.sbuf(tag + "_lgb", [128, 12], F32)
    rc = ctx.sbuf(tag + "_rc", [128, 8], F32)
    wrv = ctx.sbuf(tag + "_wr", [128, 8 * 12], BF16)
    o1r = ctx.sbuf(tag + "_o1r", [1, 128], BF16)
    rows = ctx.sbuf(tag + "_rows", [1, 3 * D + 32], BF16)
    bs1c = ctx.sbuf(tag + "_bs1c", [128, 32], F32)
    be1cv = ctx.sbuf(tag + "_be1c", [128, 48], F32)

    bs2v = rows[:, 0:D]
    be2v = rows[:, D:2 * D]
    brv = rows[:, 2 * D:2 * D + 12]

    ldg = tag + "_ldg"
    ctx.op("sync", lambda g: g.dma_start(out=o1r[:, :], in_=onesrow[:, :]), inc=((ld, 16),))
    ctx.op("gpsimd", lambda g: g.dma_start(out=bs2v, in_=bs2_r[:, :]), inc=((ldg, 16),))
    ctx.op("gpsimd", lambda g: g.dma_start(out=brv, in_=br_r[:, :]), inc=((ldg, 16),))
    ctx.op("gpsimd", lambda g: g.dma_start(
        out=wrv.rearrange("p (kc e) -> p kc e", kc=8),
        in_=wr.rearrange("(kc p) e -> p kc e", p=128)), inc=((ldg, 16),))
    ctx.op("sync", lambda y: y.dma_start(out=bs1c[:, :], in_=bs1_c[:, :]), inc=((ld, 16),))
    ws1_src = ws1.rearrange("(kc p) h -> p kc h", p=128)
    ctx.op("gpsimd", lambda g: g.dma_start(
        out=ws1v.rearrange("p (kc h) -> p kc h", kc=8), in_=ws1_src), inc=((ldg, 16),))
    ws2_src = ws2.rearrange("(kc p) d -> p kc d", p=128)
    ctx.op("gpsimd", lambda g: g.dma_start(
        out=ws2v.rearrange("p (kc d) -> p kc d", kc=16), in_=ws2_src), inc=((ldg, 16),))
    ldw = ctx.val(ld)
    ldgw = ctx.val(ldg)

    xsrc = xn_fm.rearrange("(kc p) (c t) -> c p kc t", p=128, c=NSL)
    xdst = asap(xnb).rearrange("p (kc t) -> p kc t", kc=8)
    accsrc3 = asap(acc).rearrange("p (s d) -> p s d", s=4)
    gdst = g1acc.rearrange("(c s p) d -> c p s d", c=NSL, s=4, p=128)

    def load_xn(s, mm_guard=None, eng="sync", extra=(), sem=None):
        sem = sem or xnl
        if mm_guard is None:
            mm_guard = ctx.val(pool.mm)
        ctx.op(eng, lambda y, s=s: y.dma_start(out=xdst, in_=xsrc[s]),
               wait=[(pool.mm, mm_guard)] + list(extra), inc=((sem, 16),))
        return ctx.val(sem)

    def router(s, xv):
        for mt in range(4):
            chunk = s * 4 + mt
            wcol = wall[:, chunk * 12:(chunk + 1) * 12]

            def mk(kc, mt=mt):
                return lambda t, bank: t.matmul(
                    bank[:, 0:12], xnb[:, kc * CT + mt * 128: kc * CT + mt * 128 + 128],
                    wrv[:, kc * 12:(kc + 1) * 12], start=(kc == 0), stop=False)
            mms = [mk(kc) for kc in range(8)]
            mms.append(lambda t, bank: t.matmul(bank[:, 0:12], o1r[:, :], brv,
                                                start=False, stop=True))
            pool.group(mms, lambda e, bank: e.tensor_scalar(
                lgb[:, :], bank[:, 0:12], 1.0 / 1.5, None, ALU.mult),
                first_wait=((xnl, xv), (ld, ldw), (ldg, ldgw)),
                evac_wait=((rt, ctx.val(rt)),))
            ctx.op("vector", lambda v: v.reduce_max(rc[:, 1:2], lgb[:, :], axis=AX.X),
                   wait=((pool.ev, ctx.val(pool.ev)),), inc=((rt, 1),))
            ctx.op("vector", lambda v: v.tensor_scalar(rc[:, 0:1], rc[:, 1:2], -1.0, None, ALU.mult),
                   wait=((rt, ctx.val(rt)),), inc=((rt, 1),))
            ctx.op("vector", lambda v: v.scalar_tensor_tensor(
                a_t[:, 0:12], lgb[:, :], rc[:, 1:2], lgb[:, :], ALU.is_equal, ALU.bypass),
                wait=((rt, ctx.val(rt)),), inc=((rt, 1),))
            ctx.op("vector", lambda v: v.scalar_tensor_tensor(
                a_t[:, 16:28], a_t[:, 0:12], -1e30, lgb[:, :], ALU.mult, ALU.add),
                wait=((rt, ctx.val(rt)),), inc=((rt, 1),))
            ctx.op("vector", lambda v: v.reduce_max(rc[:, 2:3], a_t[:, 16:28], axis=AX.X),
                   wait=((rt, ctx.val(rt)),), inc=((rt, 1),))
            ctx.op("vector", lambda v: v.tensor_scalar(
                rc[:, 3:4], rc[:, 2:3], rc[:, 1:2], None, ALU.subtract),
                wait=((rt, ctx.val(rt)),), inc=((rt, 1),))
            ctx.op("scalar", lambda e: e.activation(rc[:, 4:5], rc[:, 3:4], AF.Exp),
                   wait=((rt, ctx.val(rt)),), inc=((rt, 1),))
            ctx.op("vector", lambda v: v.tensor_scalar(rc[:, 5:6], rc[:, 4:5], 1.0, None, ALU.add),
                   wait=((rt, ctx.val(rt)),), inc=((rt, 1),))
            ctx.op("vector", lambda v: v.reciprocal(rc[:, 6:7], rc[:, 5:6]),
                   wait=((rt, ctx.val(rt)),), inc=((rt, 1),))
            ctx.op("scalar", lambda e: e.activation(a_t[:, 32:44], lgb[:, :], AF.Exp,
                                                    bias=rc[:, 0:1]),
                   wait=((rt, ctx.val(rt)),), inc=((rt, 1),))
            ctx.op("vector", lambda v: v.tensor_scalar(
                a_t[:, 48:60], lgb[:, :], rc[:, 2:3], None, ALU.is_ge),
                wait=((rt, ctx.val(rt)),), inc=((rt, 1),))
            ctx.op("vector", lambda v, wcol=wcol: v.scalar_tensor_tensor(
                wcol, a_t[:, 32:44], rc[:, 6:7], a_t[:, 48:60], ALU.mult, ALU.mult),
                wait=((rt, ctx.val(rt)),), inc=((rt, 1),))

    def geglu_stage(xv, nh, wv, bias_a, bias_g, Wcols, first_extra=(), xsem=None):
        xsem = xsem or xnl
        """A-stage: act[:, p*CT:(p+1)*CT] = (h_a) * gelu(h_g) for p in range(nh).
        wv: weight view; Wcols: weight free width (4096 dense / 6144 expert)."""
        for p in range(nh):
            par = 0
            tw = [(tt, ctx.val(tt))] if ctx.val(tt) >= 1 else []
            if p == 0:
                tw = tw + [(rt, ctx.val(rt))]

            def mka(kc, p=p):
                return lambda t, bank: t.matmul(
                    bank[:, :], wv[:, kc * Wcols + p * 128: kc * Wcols + p * 128 + 128],
                    xnb[:, kc * CT:(kc + 1) * CT], start=(kc == 0), stop=(kc == 7))

            def mkg(kc, p=p, nh=nh):
                return lambda t, bank: t.matmul(
                    bank[:, :], wv[:, kc * Wcols + (nh + p) * 128: kc * Wcols + (nh + p) * 128 + 128],
                    xnb[:, kc * CT:(kc + 1) * CT], start=(kc == 0), stop=(kc == 7))
            pool.group([mka(kc) for kc in range(8)],
                       lambda e, bank, p=p, par=par: e.activation(
                           a_t[:, par * CT:(par + 1) * CT], bank[:, :], AF.Identity,
                           bias=bias_a[:, p:p + 1]),
                       evac_eng="scalar",
                       first_wait=[(xsem, xv), (ld, ldw), (ldg, ldgw)] + list(first_extra),
                       evac_wait=tw)
            pool.group([mkg(kc) for kc in range(8)],
                       lambda e, bank, p=p, par=par: e.activation(
                           g_t[:, par * CT:(par + 1) * CT], bank[:, :], AF.Gelu,
                           bias=bias_g[:, p:p + 1]),
                       evac_eng="scalar", evac_wait=tw)
            ctx.op("vector", lambda v, p=p, par=par: v.tensor_mul(
                act[:, p * CT:(p + 1) * CT], a_t[:, par * CT:(par + 1) * CT],
                g_t[:, par * CT:(par + 1) * CT]),
                wait=((pool.ev, ctx.val(pool.ev)),), inc=((tt, 1),))

    def second_stage(s, nkc, wv2, Wcols2, brow, accum_expert=None, store_guard=()):
        ttv = ctx.val(tt)
        first = True
        for mt in range(4):
            for dh in range(2):
                def mk(kc, mt=mt, dh=dh):
                    return lambda t, bank: t.matmul(
                        bank[:, :], act[:, kc * CT + mt * 128: kc * CT + mt * 128 + 128],
                        wv2[:, kc * Wcols2 + dh * 512: kc * Wcols2 + (dh + 1) * 512],
                        start=(kc == 0), stop=False)
                mms = [mk(kc) for kc in range(nkc)]
                mms.append(lambda t, bank, dh=dh: t.matmul(
                    bank[:, :], o1r[:, :], brow[:, dh * 512:(dh + 1) * 512],
                    start=False, stop=True))
                dst = acc[:, mt * D + dh * 512: mt * D + (dh + 1) * 512]
                ew = tuple(store_guard) if first else ()
                first = False
                if accum_expert is None:
                    pool.group(mms, lambda e, bank, dst=dst: e.tensor_copy(dst, bank[:, :]),
                               first_wait=((tt, ttv),), evac_wait=ew)
                else:
                    e_, s_ = accum_expert, s
                    wcol = wall[:, (s_ * 4 + mt) * 12 + e_: (s_ * 4 + mt) * 12 + e_ + 1]
                    pool.group(mms, lambda e, bank, dst=dst, wcol=wcol:
                               e.scalar_tensor_tensor(dst, bank[:, :], wcol, dst,
                                                      ALU.mult, ALU.add),
                               first_wait=((tt, ttv),), evac_wait=ew)

    prev_mmA = 0
    for s in range(NSL):
        xv = load_xn(s, mm_guard=prev_mmA)
        router(s, xv)
        geglu_stage(xv, 16, ws1v, bs1c[:, 0:16], bs1c[:, 16:32], 4096)
        prev_mmA = ctx.val(pool.mm)
        second_stage(s, 16, ws2v, 1024, bs2v, store_guard=((st, ctx.val(st)),))
        ctx.op("sync", lambda y, s=s: y.dma_start(out=gdst[s], in_=accsrc3),
               wait=((pool.ev, ctx.val(pool.ev)),), inc=((st, 16),))

    if sparse is None:
        for e_ in range(12):
            mmv = ctx.val(pool.mm)
            w1guard = mmv if e_ == 0 else prev_mmA
            we1_src = we1[e_].rearrange("(kc p) h -> p kc h", p=128)
            we1_dst = we1v.rearrange("p (kc h) -> p kc h", kc=8)
            for kc in range(8):
                ctx.op("gpsimd", lambda g, kc=kc, we1_src=we1_src, we1_dst=we1_dst:
                       g.dma_start(out=we1_dst[:, kc], in_=we1_src[:, kc]),
                       wait=([(pool.mm, w1guard)] + list(wload_wait)) if kc == 0 else (),
                       inc=((wld, 16),))
            we2_src = we2[e_].rearrange("(kc p) d -> p kc d", p=128)
            we2_dst = we2v.rearrange("p (kc d) -> p kc d", kc=24)
            for g4 in range(4):
                ctx.op("gpsimd", lambda g, g4=g4, we2_src=we2_src, we2_dst=we2_dst:
                       g.dma_start(out=we2_dst[:, g4 * 6:(g4 + 1) * 6],
                                   in_=we2_src[:, g4 * 6:(g4 + 1) * 6]),
                       wait=((pool.mm, mmv),) if g4 == 0 else (),
                       inc=((wld, 16),))
            ctx.op("sync", lambda y, e_=e_: y.dma_start(out=be1cv[:, :], in_=be1_c[e_]),
                   wait=((pool.mm, mmv),), inc=((tag + "_wlds", 16),))
            ctx.op("gpsimd", lambda g, e_=e_: g.dma_start(out=be2v, in_=be2_r[e_]),
                   inc=((wld, 16),))
            wldv = ctx.val(wld)
            wldsv = ctx.val(tag + "_wlds")
            for s in range(NSL):
                ctx.op("gpsimd", lambda g, s=s: g.dma_start(out=accsrc3, in_=gdst[s]),
                       wait=((st, ctx.val(st)), (pool.mm, prev_mmA)), inc=((tag + "_xng", 16),))
                xv = load_xn(s, mm_guard=prev_mmA, eng="gpsimd", sem=tag + "_xng")
                geglu_stage(xv, 24, we1v, be1cv[:, 0:24], be1cv[:, 24:48], 6144,
                            first_extra=((wld, wldv), (tag + "_wlds", wldsv)),
                            xsem=tag + "_xng")
                prev_mmA = ctx.val(pool.mm)
                second_stage(s, 24, we2v, 1024, be2v, accum_expert=e_)
                ctx.op("sync", lambda y, s=s: y.dma_start(out=gdst[s], in_=accsrc3),
                       wait=((pool.ev, ctx.val(pool.ev)),), inc=((st, 16),))


    else:
        CAP = sparse["CAP"]
        xn_tmb, xcomp, yew, wcomp = (sparse["xn_tmb"], sparse["xcomp"],
                                     sparse["yew"], sparse["wcomp"])
        NOC = CAP // 128
        ig, sca = tag + "_ig", tag + "_sca"
        ysts = tag + "_yst"
        with ctx.scope():
            posb = carve(war, 122880, 512, F32, parts=32)
            z32 = carve(war, 124928, 128, F32, parts=32)
            r32 = carve(war, 125440, 128, F32, parts=1)
            exc = carve(war, 125984, 1, F32, parts=32)
            posT = carve(war, 126016, 32, F32)
            posTi = ctx.sbuf(tag + "_pti", [128, 12 * 32], mybir.dt.int32)
            xch = ctx.sbuf(tag + "_xch", [128, D], BF16)
            xgT = ctx.sbuf(tag + "_xgT", [128, 8 * CAP], BF16)
            stg3 = ctx.sbuf(tag + "_stg3", [128, D], BF16)
            yst = ctx.sbuf(tag + "_yst", [128, 2 * 512], F32)
            yg = carve(war, 65536, 2 * D, F32)
            idfv = ctx.sbuf(tag + "_idfv", [128, 128], F32)
            idbv = ctx.sbuf(tag + "_idbv", [128, 128], BF16)
            tps2 = [ctx.psum(tag + f"_tq{i}", [128, 128], BF16) for i in range(2)]
            ctx.op("sync", lambda y: y.dma_start(out=idfv[:, :], in_=cid_f32[:, :]),
                   inc=((ld, 16),))
            ctx.op("sync", lambda y: y.dma_start(out=idbv[:, :], in_=cid_bf[:, :]),
                   inc=((ld, 16),))
            idv = ctx.val(ld)

            denv = ctx.val(pool.ev)
            rtv = ctx.val(rt)
            wall3 = wall.ap().rearrange("p (c e) -> p c e", e=12)
            ctx.op("vector", lambda v: v.memset(z32[:, :], 0.0), inc=((sca, 1),))
            ctx.op("vector", lambda v: v.memset(r32[:, 96:128], 0.0), inc=((sca, 1),))
            for e_ in range(12):
                pool.group([lambda t, bank, e_=e_: t.transpose(
                    bank[0:32, 0:128], wall3[:, :, e_], idfv[:, :])],
                    lambda e, bank: e.tensor_copy(posb[:, 0:128], bank[0:32, 0:128]),
                    first_wait=((rt, rtv), (pool.ev, denv), (ld, idv),
                                (sca, ctx.val(sca))))
                ctx.op("vector", lambda v: v.tensor_scalar(
                    posb[:, 128:256], posb[:, 0:128], 0.0, None, ALU.is_gt),
                    wait=((pool.ev, ctx.val(pool.ev)),), inc=((sca, 1),))
                ctx.op("vector", lambda v: v.tensor_tensor_scan(
                    posb[:, 256:384], posb[:, 128:256], z32[:, :], 0.0, ALU.add, ALU.add),
                    wait=((sca, ctx.val(sca)),), inc=((sca, 1),))
                pool.group([lambda t, bank: t.transpose(
                    bank[0:1, 0:32], posb[:, 383:384], idfv[0:32, 0:32])],
                    lambda e, bank: e.tensor_copy(r32[:, 0:32], bank[0:1, 0:32]),
                    first_wait=((sca, ctx.val(sca)),))
                ctx.op("vector", lambda v: v.tensor_tensor_scan(
                    r32[:, 32:64], r32[:, 0:32], r32[:, 96:128], 0.0, ALU.add, ALU.add),
                    wait=((pool.ev, ctx.val(pool.ev)),), inc=((sca, 1),))
                ctx.op("vector", lambda v: v.tensor_sub(r32[:, 64:96], r32[:, 32:64], r32[:, 0:32]),
                       wait=((sca, ctx.val(sca)),), inc=((sca, 1),))
                pool.group([lambda t, bank: t.transpose(
                    bank[0:32, 0:1], r32[:, 64:96], idfv[0:1, 0:1])],
                    lambda e, bank: e.tensor_copy(exc[:, :], bank[0:32, 0:1]),
                    first_wait=((sca, ctx.val(sca)),))
                ctx.op("vector", lambda v: v.tensor_scalar(
                    posb[:, 256:384], posb[:, 256:384], exc[:, 0:1], -1.0, ALU.add, ALU.add),
                    wait=((pool.ev, ctx.val(pool.ev)),), inc=((sca, 1),))
                ctx.op("vector", lambda v: v.tensor_scalar(
                    posb[:, 0:128], posb[:, 128:256], -1.0, 1.0, ALU.mult, ALU.add),
                    wait=((sca, ctx.val(sca)),), inc=((sca, 1),))
                ctx.op("vector", lambda v: v.scalar_tensor_tensor(
                    posb[:, 0:128], posb[:, 0:128], 1.0e7, posb[:, 256:384],
                    ALU.mult, ALU.add),
                    wait=((sca, ctx.val(sca)),), inc=((sca, 1),))
                pool.group([lambda t, bank: t.transpose(
                    bank[0:128, 0:32], posb[:, 0:128], idfv[0:32, 0:32])],
                    lambda e, bank: e.tensor_copy(posT[:, :], bank[0:128, 0:32]),
                    first_wait=((sca, ctx.val(sca)),))
                ctx.op("vector", lambda v, e_=e_: v.tensor_copy(
                    posTi[:, e_ * 32:(e_ + 1) * 32], posT[:, :]),
                    wait=((pool.ev, ctx.val(pool.ev)),), inc=((sca, 1),))
            scv = ctx.val(sca)

            xcf = xcomp
            ywf = yew
            xcf3 = xcomp.rearrange("(z p) d -> z p d", p=128)
            xnsrc = xn_tmb.rearrange("(c p) d -> c p d", p=128)
            ctx.op("vector", lambda v: v.memset(xch[:, :], 0.0), inc=((sca, 1),))
            zv = ctx.val(sca)
            for z in range(12 * (CAP // 128)):
                ctx.op("gpsimd", lambda g, z=z: g.dma_start(out=xcf3[z], in_=xch[:, :]),
                       wait=((sca, zv),) if z == 0 else (), inc=((ig, 16),))
            for c in range(32):
                ctx.op("gpsimd", lambda g, c=c: g.dma_start(out=xch[:, :], in_=xnsrc[c]),
                       wait=[(sca, scv), (ig, ctx.val(ig))],
                       inc=((ig, 16),))
                igv = ctx.val(ig)
                for e_ in range(12):
                    ctx.op("gpsimd", lambda g, c=c, e_=e_: g.indirect_dma_start(
                        xcf, IndirectOffsetOnAxis(posTi[:, e_ * 32 + c: e_ * 32 + c + 1], 0),
                        xch[:, :], None, element_offset=e_ * CAP * D,
                        bounds_check=CAP - 1, oob_is_err=False),
                        wait=((ig, igv),) if e_ == 0 else (), inc=((ig, 16),))
            scat_done = ctx.val(ig)

            lsems3 = (tag + "_l3", tag + "_t3", tag + "_e3")
            for e_ in range(12):
                mmv = ctx.val(pool.mm)
                sX, vX = emit_load_w_T(ctx, tag + f"xg{e_}",
                                       xcomp[e_ * CAP:(e_ + 1) * CAP, :], CAP, D,
                                       xgT, idbv, stg3, tps2,
                                       after=[(ig, scat_done), (pool.mm, mmv)],
                                       sems=lsems3, nbuf=1)
                we1_src = we1[e_].rearrange("(kc p) h -> p kc h", p=128)
                we1_dst = we1v.rearrange("p (kc h) -> p kc h", kc=8)
                for kc in range(8):
                    ctx.op("gpsimd", lambda g, kc=kc, we1_src=we1_src, we1_dst=we1_dst:
                           g.dma_start(out=we1_dst[:, kc], in_=we1_src[:, kc]),
                           wait=([(pool.mm, mmv)] + list(wload_wait)) if kc == 0 else (),
                           inc=((wld, 16),))
                we2_src = we2[e_].rearrange("(kc p) d -> p kc d", p=128)
                we2_dst = we2v.rearrange("p (kc d) -> p kc d", kc=24)
                for g4 in range(4):
                    ctx.op("gpsimd", lambda g, g4=g4, we2_src=we2_src, we2_dst=we2_dst:
                           g.dma_start(out=we2_dst[:, g4 * 6:(g4 + 1) * 6],
                                       in_=we2_src[:, g4 * 6:(g4 + 1) * 6]),
                           inc=((wld, 16),))
                ctx.op("sync", lambda y, e_=e_: y.dma_start(out=be1cv[:, :], in_=be1_c[e_]),
                       wait=((pool.mm, mmv),), inc=((tag + "_wlds", 16),))
                ctx.op("gpsimd", lambda g, e_=e_: g.dma_start(out=be2v, in_=be2_r[e_]),
                       inc=((wld, 16),))
                wldv = ctx.val(wld)
                wldsv = ctx.val(tag + "_wlds")
                for s3 in range(CAP // 512):
                    for p in range(24):
                        tw = [(tt, ctx.val(tt))] if ctx.val(tt) >= 1 else []

                        def mka(kc, p=p, s3=s3):
                            return lambda t, bank: t.matmul(
                                bank[:, :], we1v[:, kc * 6144 + p * 128: kc * 6144 + p * 128 + 128],
                                xgT[:, kc * CAP + s3 * 512: kc * CAP + s3 * 512 + 512],
                                start=(kc == 0), stop=(kc == 7))

                        def mkg(kc, p=p, s3=s3):
                            return lambda t, bank: t.matmul(
                                bank[:, :], we1v[:, kc * 6144 + (24 + p) * 128: kc * 6144 + (24 + p) * 128 + 128],
                                xgT[:, kc * CAP + s3 * 512: kc * CAP + s3 * 512 + 512],
                                start=(kc == 0), stop=(kc == 7))
                        pool.group([mka(kc) for kc in range(8)],
                                   lambda e, bank, p=p: e.activation(
                                       a_t[:, 0:512], bank[:, :], AF.Identity,
                                       bias=be1cv[:, p:p + 1]),
                                   evac_eng="scalar",
                                   first_wait=[(sX, vX), (wld, wldv),
                                               (tag + "_wlds", wldsv)] if p == 0 else [],
                                   evac_wait=tw)
                        pool.group([mkg(kc) for kc in range(8)],
                                   lambda e, bank, p=p: e.activation(
                                       g_t[:, 0:512], bank[:, :], AF.Gelu,
                                       bias=be1cv[:, 24 + p:25 + p]),
                                   evac_eng="scalar", evac_wait=tw)
                        ctx.op("vector", lambda v, p=p: v.tensor_mul(
                            act[:, p * 512:(p + 1) * 512], a_t[:, 0:512], g_t[:, 0:512]),
                            wait=((pool.ev, ctx.val(pool.ev)),), inc=((tt, 1),))
                    ttv = ctx.val(tt)
                    for tc in range(4):
                        for dh in range(2):
                            def mkb(kc, tc=tc, dh=dh):
                                return lambda t, bank: t.matmul(
                                    bank[:, :], act[:, kc * 512 + tc * 128: kc * 512 + tc * 128 + 128],
                                    we2v[:, kc * D + dh * 512: kc * D + (dh + 1) * 512],
                                    start=(kc == 0), stop=False)
                            mms = [mkb(kc) for kc in range(24)]
                            mms.append(lambda t, bank, dh=dh: t.matmul(
                                bank[:, :], o1r[:, :], be2v[:, dh * 512:(dh + 1) * 512],
                                start=False, stop=True))
                            dst = yst[:, dh * 512:(dh + 1) * 512]
                            pool.group(mms, lambda e, bank, dst=dst:
                                       e.tensor_copy(dst, bank[:, :]),
                                       first_wait=((tt, ttv),),
                                       evac_wait=((ysts, ctx.val(ysts)),))
                        ctx.op("sync", lambda y, e_=e_, s3=s3, tc=tc: y.dma_start(
                            out=yew.rearrange("(z p) d -> z p d", p=128)[e_ * 8 + s3 * 4 + tc],
                            in_=yst[:, :]),
                            wait=((pool.ev, ctx.val(pool.ev)),), inc=((ysts, 16),))

            ysf = ctx.val(ysts)
            mmall = ctx.val(pool.mm)
            ep = [carve(war, i * 8192, D, F32) for i in range(6)]
            eo = [carve(war, 49152 + i * 8192, D, F32) for i in range(2)]
            g1src = g1acc.rearrange("(c p) d -> c p d", p=128)
            x1src = x1_tm.rearrange("(c p) d -> c p d", p=128)
            s1src = s1_tm.rearrange("(c p) d -> c p d", p=128)
            osrc = out_tm.rearrange("(c p) d -> c p d", p=128)
            epl = tag + "_epl"
            for c in range(32):
                par = c % 2
                gt, x1t, s1t = ep[par * 3], ep[par * 3 + 1], ep[par * 3 + 2]
                ot = eo[par]
                bw = [(rt, ctx.val(rt))] if c >= 2 else [(pool.mm, mmall), (ysts, ysf)]
                ctx.op("sync", lambda y, c=c, gt=gt: y.dma_start(out=gt, in_=g1src[c]),
                       wait=bw + ([(st, ctx.val(st))] if c >= 2 else []), inc=((epl, 16),))
                ctx.op("sync", lambda y, c=c, x1t=x1t: y.dma_start(out=x1t, in_=x1src[c]),
                       inc=((epl, 16),))
                ctx.op("sync", lambda y, c=c, s1t=s1t: y.dma_start(out=s1t, in_=s1src[c]),
                       inc=((epl, 16),))
                eplv = ctx.val(epl)
                for e_ in range(12):
                    gp = e_ % 2
                    gw = [(rt, ctx.val(rt) - 1)] if ctx.val(rt) >= 1 and e_ >= 2 else []
                    if e_ == 0:
                        gw = [(ysts, ysf), (rt, ctx.val(rt))]
                    ctx.op("gpsimd", lambda g, c=c, e_=e_, gp=gp: g.indirect_dma_start(
                        yg[:, gp * D:(gp + 1) * D], None, ywf,
                        IndirectOffsetOnAxis(posTi[:, e_ * 32 + c: e_ * 32 + c + 1], 0),
                        element_offset=e_ * CAP * D,
                        bounds_check=CAP - 1, oob_is_err=False),
                        wait=gw, inc=((ig, 16),))
                    ctx.op("vector", lambda v, c=c, e_=e_, gp=gp, gt=gt: v.scalar_tensor_tensor(
                        gt, yg[:, gp * D:(gp + 1) * D], wall3[:, c, e_:e_ + 1], gt,
                        ALU.mult, ALU.add),
                        wait=((ig, ctx.val(ig)), (epl, eplv)), inc=((rt, 1),))
                ctx.op("scalar", lambda e, gt=gt, ot=ot: e.activation(ot, gt, AF.Silu),
                       wait=((rt, ctx.val(rt)), (epl, eplv)), inc=((rt, 1),))
                ctx.op("vector", lambda v, ot=ot, s1t=s1t: v.tensor_mul(ot, ot, s1t),
                       wait=((rt, ctx.val(rt)),), inc=((rt, 1),))
                ctx.op("vector", lambda v, ot=ot, x1t=x1t: v.tensor_add(ot, ot, x1t),
                       wait=((rt, ctx.val(rt)),), inc=((rt, 1),))
                ctx.op("sync", lambda y, c=c, ot=ot: y.dma_start(out=osrc[c], in_=ot),
                       wait=((rt, ctx.val(rt)),), inc=((st, 16),))
        ctx.wait("sync", (st, ctx.val(st)))
        return

    xsrc1 = x1_tm.rearrange("(c s p) d -> c p s d", c=NSL, s=4, p=128)
    ssrc1 = s1_tm.rearrange("(c s p) d -> c p s d", c=NSL, s=4, p=128)
    odst = out_tm.rearrange("(c s p) d -> c p s d", c=NSL, s=4, p=128)
    epi3 = [v.rearrange("p (s d) -> p s d", s=4) for v in epi]
    mmv = ctx.val(pool.mm)
    for s in range(NSL):
        ctx.op("sync", lambda y, s=s: y.dma_start(out=accsrc3, in_=gdst[s]),
               wait=((pool.ev, ctx.val(pool.ev)), (pool.mm, mmv)), inc=((xnl, 16),))
        ctx.op("sync", lambda y, s=s: y.dma_start(out=epi3[0], in_=xsrc1[s]),
               wait=((rt, ctx.val(rt)),), inc=((xnl, 16),))
        ctx.op("sync", lambda y, s=s: y.dma_start(out=epi3[1], in_=ssrc1[s]),
               inc=((xnl, 16),))
        xnv = ctx.val(xnl)
        ctx.op("scalar", lambda e: e.activation(epi[2], acc[:, :], AF.Silu),
               wait=((xnl, xnv),), inc=((rt, 1),))
        ctx.op("vector", lambda v: v.tensor_mul(epi[3], epi[2], epi[1]),
               wait=((rt, ctx.val(rt)),), inc=((rt, 1),))
        ctx.op("vector", lambda v: v.tensor_add(epi[3], epi[3], epi[0]),
               wait=((rt, ctx.val(rt)),), inc=((rt, 1),))
        ctx.op("sync", lambda y, s=s: y.dma_start(out=odst[s], in_=epi3[3]),
               wait=((rt, ctx.val(rt)),), inc=((st, 16),))
    ctx.wait("sync", (st, ctx.val(st)))


def build_p0(nchunks=8):
    nc = bass.Bass("TRN2", target_bir_lowering=False, debug=False)
    stack = ExitStack()
    u = nc.dram_tensor("u", [L, D], F32, kind="ExternalInput").ap()
    cid = nc.dram_tensor("cid_bf", [128, 128], BF16, kind="ExternalInput").ap()
    cc = nc.dram_tensor("cc", [128, Ctx.NCONST], F32, kind="ExternalInput").ap()
    xn = nc.dram_tensor("xn_fm", [D, L], BF16, kind="ExternalOutput").ap()
    xntm = nc.dram_tensor("xn_tm", [L, D], F32, kind="ExternalOutput").ap()
    with stack:
        ctx = Ctx(nc, stack)
        ctx.emit_const_load(cc)
        emit_ln_fm(ctx, u, xn, cid, "p0", nchunks=nchunks, also_tm=xntm)
        ctx.assemble()
    return nc, {"cc": ctx.const_array()}


def build_p3():
    nc = bass.Bass("TRN2", target_bir_lowering=False, debug=False)
    stack = ExitStack()
    xn = nc.dram_tensor("xn_fm", [D, L], BF16, kind="ExternalInput").ap()
    wq = nc.dram_tensor("Wq", [D, D], F32, kind="ExternalInput").ap()
    wk = nc.dram_tensor("Wk", [D, D], F32, kind="ExternalInput").ap()
    wv = nc.dram_tensor("Wv", [D, D], F32, kind="ExternalInput").ap()
    bq_c = nc.dram_tensor("bq_c", [128, 8], F32, kind="ExternalInput").ap()
    bk_c = nc.dram_tensor("bk_c", [128, 8], F32, kind="ExternalInput").ap()
    bv_r = nc.dram_tensor("bv_r", [1, D], F32, kind="ExternalInput").ap()
    lqk = nc.dram_tensor("lqk", [16, 256], F32, kind="ExternalInput").ap()
    c16x8 = nc.dram_tensor("c16x8", [16, 8], F32, kind="ExternalInput").ap()
    cB16 = nc.dram_tensor("cB16", [16, 128], F32, kind="ExternalInput").ap()
    onesrow = nc.dram_tensor("onesrow", [1, 128], BF16, kind="ExternalInput").ap()
    q = nc.dram_tensor("q_fm", [D, L], BF16, kind="ExternalOutput").ap()
    k = nc.dram_tensor("k_fm", [D, L], BF16, kind="ExternalOutput").ap()
    v = nc.dram_tensor("v_tm", [L, D], BF16, kind="ExternalOutput").ap()
    with stack:
        ctx = Ctx(nc, stack)
        emit_qkv(ctx, "p3", xn, wq, wk, wv, bq_c, bk_c, lqk, c16x8, cB16,
                 onesrow, bv_r, q, k, v)
        ctx.assemble()
    return nc


def build_p5():
    nc = bass.Bass("TRN2", target_bir_lowering=False, debug=False)
    stack = ExitStack()
    o = nc.dram_tensor("o_fm", [D, L], BF16, kind="ExternalInput").ap()
    wo = nc.dram_tensor("Wo", [D, D], F32, kind="ExternalInput").ap()
    bo_r = nc.dram_tensor("bo_r", [1, D], F32, kind="ExternalInput").ap()
    onesrow = nc.dram_tensor("onesrow", [1, 128], BF16, kind="ExternalInput").ap()
    s1 = nc.dram_tensor("s1_tm", [L, D], F32, kind="ExternalOutput").ap()
    with stack:
        ctx = Ctx(nc, stack)
        emit_proj_tm(ctx, "p5", o, wo, bo_r, s1, D, D, onesrow)
        ctx.assemble()
    return nc


def build_p1():
    nc = bass.Bass("TRN2", target_bir_lowering=False, debug=False)
    stack = ExitStack()
    xn = nc.dram_tensor("xn_fm", [D, L], BF16, kind="ExternalInput").ap()
    ipw = nc.dram_tensor("in_proj_w", [4384, D], F32, kind="ExternalInput").ap()
    opw = nc.dram_tensor("out_proj_w", [D, 2048], F32, kind="ExternalInput").ap()
    cwc = nc.dram_tensor("conv_w_c", [128, 72], F32, kind="ExternalInput").ap()
    cbc = nc.dram_tensor("conv_b_c", [128, 18], F32, kind="ExternalInput").ap()
    dtb = nc.dram_tensor("dtb_c", [32, 1], F32, kind="ExternalInput").ap()
    alg = nc.dram_tensor("alog_c", [32, 1], F32, kind="ExternalInput").ap()
    drp = nc.dram_tensor("d_rep", [128, 16], F32, kind="ExternalInput").ap()
    nwc = nc.dram_tensor("nw_c", [128, 16], F32, kind="ExternalInput").ap()
    tmk = nc.dram_tensor("tmask", [128, 384], F32, kind="ExternalInput").ap()
    cidb = nc.dram_tensor("cid_bf", [128, 128], BF16, kind="ExternalInput").ap()
    cidf = nc.dram_tensor("cid_f32", [128, 128], F32, kind="ExternalInput").ap()
    cc = nc.dram_tensor("cc", [128, Ctx.NCONST], F32, kind="ExternalInput").ap()
    s0 = nc.dram_tensor("s0_tm", [L, D], F32, kind="ExternalOutput").ap()
    with stack:
        ctx = Ctx(nc, stack)
        ctx.emit_const_load(cc)
        emit_mamba(ctx, "p1", xn, s0, ipw, opw, cwc, cbc, dtb, alg, drp, nwc,
                   tmk, cidb, cidf)
        ctx.assemble()
    return nc, ctx


def p1_host_inputs(inp, consts):
    conv_w = np.asarray(inp["L0_attn_conv_w"], np.float32)
    d = {}
    d["in_proj_w"] = np.asarray(inp["L0_attn_in_proj_w"], np.float32)
    d["out_proj_w"] = np.asarray(inp["L0_attn_out_proj_w"], np.float32)
    d["conv_w_c"] = np.concatenate([col_layout(conv_w[:, k]) for k in range(4)], axis=1)
    d["conv_b_c"] = col_layout(np.asarray(inp["L0_attn_conv_b"], np.float32))
    d["dtb_c"] = np.asarray(inp["L0_attn_dt_bias"], np.float32)[:, None]
    d["alog_c"] = np.asarray(inp["L0_attn_A_log"], np.float32)[:, None]
    d["d_rep"] = col_layout(np.repeat(np.asarray(inp["L0_attn_D"], np.float32), 64))
    d["nw_c"] = col_layout(np.asarray(inp["L0_attn_norm_w"], np.float32))
    j = np.arange(128)[:, None]
    ip = np.arange(256)[None, :]
    d["tmask"] = np.concatenate([np.zeros((128, 128), np.float32),
                                 (ip >= j).astype(np.float32)], axis=1)
    d["cid_bf"] = consts["cid_bf"]
    d["cid_f32"] = consts["cid_f32"]
    return d


def build_p2():
    nc = bass.Bass("TRN2", target_bir_lowering=False, debug=False)
    stack = ExitStack()
    xn = nc.dram_tensor("xn_fm", [D, L], BF16, kind="ExternalInput").ap()
    u = nc.dram_tensor("u_tm", [L, D], F32, kind="ExternalInput").ap()
    s0 = nc.dram_tensor("s0_tm", [L, D], F32, kind="ExternalInput").ap()
    gate = nc.dram_tensor("gate_w", [8, D], F32, kind="ExternalInput").ap()
    b1c = nc.dram_tensor("b1_c", [8, 128, 32], F32, kind="ExternalInput").ap()
    b2r = nc.dram_tensor("b2_r", [8, 1, D], F32, kind="ExternalInput").ap()
    w1 = nc.dram_tensor("w1", [4096, 8, D], F32, kind="ExternalInput").ap()
    w2 = nc.dram_tensor("w2", [D, 8, 4096], F32, kind="ExternalInput").ap()
    cidb = nc.dram_tensor("cid_bf", [128, 128], BF16, kind="ExternalInput").ap()
    onesrow = nc.dram_tensor("onesrow", [1, 128], BF16, kind="ExternalInput").ap()
    x1 = nc.dram_tensor("x1_tm", [L, D], F32, kind="ExternalOutput").ap()
    g0acc = nc.dram_tensor("g0acc", [L, D], F32, kind="Internal").ap()
    with stack:
        ctx = Ctx(nc, stack)
        emit_l0moe(ctx, "p2", xn, u, s0, x1, g0acc, gate, b1c, b2r, w1, w2,
                   cidb, onesrow)
        ctx.assemble()
    return nc, ctx


def build_p4():
    nc = bass.Bass("TRN2", target_bir_lowering=False, debug=False)
    stack = ExitStack()
    q = nc.dram_tensor("q_fm", [D, L], BF16, kind="ExternalInput").ap()
    k = nc.dram_tensor("k_fm", [D, L], BF16, kind="ExternalInput").ap()
    v = nc.dram_tensor("v_tm", [L, D], BF16, kind="ExternalInput").ap()
    gnw = nc.dram_tensor("gnw_c", [64, 16], F32, kind="ExternalInput").ap()
    gnb = nc.dram_tensor("gnb_c", [64, 16], F32, kind="ExternalInput").ap()
    cidf = nc.dram_tensor("cid_f32", [128, 128], F32, kind="ExternalInput").ap()
    onesrow = nc.dram_tensor("onesrow", [1, 128], BF16, kind="ExternalInput").ap()
    cc = nc.dram_tensor("cc", [128, Ctx.NCONST], F32, kind="ExternalInput").ap()
    o = nc.dram_tensor("o_fm", [D, L], BF16, kind="ExternalOutput").ap()
    mrow = nc.dram_tensor("mrow", [32, 128], F32, kind="Internal").ap()
    with stack:
        ctx = Ctx(nc, stack)
        ctx.emit_const_load(cc)
        emit_attn(ctx, "p4", q, k, v, gnw, gnb, cidf, onesrow, mrow, o)
        ctx.assemble()
    return nc, ctx


def build_p6():
    nc = bass.Bass("TRN2", target_bir_lowering=False, debug=False)
    stack = ExitStack()
    xn = nc.dram_tensor("xn_fm", [D, L], BF16, kind="ExternalInput").ap()
    x1 = nc.dram_tensor("x1_tm", [L, D], F32, kind="ExternalInput").ap()
    s1 = nc.dram_tensor("s1_tm", [L, D], F32, kind="ExternalInput").ap()
    ws1 = nc.dram_tensor("Ws1", [D, 4096], F32, kind="ExternalInput").ap()
    bs1c = nc.dram_tensor("bs1_c", [128, 32], F32, kind="ExternalInput").ap()
    ws2 = nc.dram_tensor("Ws2", [2048, D], F32, kind="ExternalInput").ap()
    bs2r = nc.dram_tensor("bs2_r", [1, D], F32, kind="ExternalInput").ap()
    we1 = nc.dram_tensor("We1", [12, D, 6144], F32, kind="ExternalInput").ap()
    be1c = nc.dram_tensor("be1_c", [12, 128, 48], F32, kind="ExternalInput").ap()
    we2 = nc.dram_tensor("We2", [12, 3072, D], F32, kind="ExternalInput").ap()
    be2r = nc.dram_tensor("be2_r", [12, 1, D], F32, kind="ExternalInput").ap()
    wr = nc.dram_tensor("Wr", [D, 12], F32, kind="ExternalInput").ap()
    brr = nc.dram_tensor("br_r", [1, 12], F32, kind="ExternalInput").ap()
    onesrow = nc.dram_tensor("onesrow", [1, 128], BF16, kind="ExternalInput").ap()
    cidf = nc.dram_tensor("cid_f32", [128, 128], F32, kind="ExternalInput").ap()
    cidb = nc.dram_tensor("cid_bf", [128, 128], BF16, kind="ExternalInput").ap()
    xn1tb = nc.dram_tensor("xn_tmb", [L, D], BF16, kind="ExternalInput").ap()
    out = nc.dram_tensor("out_tm", [L, D], F32, kind="ExternalOutput").ap()
    g1acc = nc.dram_tensor("g1acc", [L, D], F32, kind="Internal").ap()
    CAP = 1024
    xcomp = nc.dram_tensor("xcomp", [12 * CAP, D], BF16, kind="Internal").ap()
    yewt = nc.dram_tensor("yew", [12 * CAP, D], F32, kind="Internal").ap()
    wcomp = nc.dram_tensor("wcomp", [12 * CAP, 1], F32, kind="Internal").ap()
    with stack:
        ctx = Ctx(nc, stack)
        emit_l1ffn(ctx, "p6", xn, x1, s1, out, g1acc, ws1, bs1c, bs2r, ws2,
                   we1, be1c, we2, be2r, wr, brr, onesrow, cid_f32=cidf,
                   cid_bf=cidb, sparse={"CAP": CAP, "xn_tmb": xn1tb, "xcomp": xcomp,
                           "yew": yewt, "wcomp": wcomp})
        ctx.assemble()
    return nc


def build_ln(name_in, name_out):
    nc = bass.Bass("TRN2", target_bir_lowering=False, debug=False)
    stack = ExitStack()
    u = nc.dram_tensor(name_in, [L, D], F32, kind="ExternalInput").ap()
    cid = nc.dram_tensor("cid_bf", [128, 128], BF16, kind="ExternalInput").ap()
    cc = nc.dram_tensor("cc", [128, Ctx.NCONST], F32, kind="ExternalInput").ap()
    xn = nc.dram_tensor(name_out, [D, L], BF16, kind="ExternalOutput").ap()
    with stack:
        ctx = Ctx(nc, stack)
        ctx.emit_const_load(cc)
        emit_ln_fm(ctx, u, xn, cid, "ln")
        ctx.assemble()
    return nc, ctx


def _f32(x):
    return np.ascontiguousarray(np.asarray(x, dtype=np.float32))


def build_fused():
    nc = bass.Bass("TRN2", target_bir_lowering=False, debug=False)
    stack = ExitStack()
    EI = lambda n, shp, dt=F32: nc.dram_tensor(n, shp, dt, kind="ExternalInput").ap()
    IN = lambda n, shp, dt=F32: nc.dram_tensor(n, shp, dt, kind="Internal").ap()
    u = EI("u", [L, D])
    cc = EI("cc", [128, Ctx.NCONST])
    cidb = EI("cid_bf", [128, 128], BF16)
    cidf = EI("cid_f32", [128, 128])
    onesrow = EI("onesrow", [1, 128], BF16)
    ipw = EI("in_proj_w", [4384, D]); opw = EI("out_proj_w", [D, 2048])
    cwc = EI("conv_w_c", [128, 72]); cbc = EI("conv_b_c", [128, 18])
    dtb = EI("dtb_c", [32, 1]); alg = EI("alog_c", [32, 1])
    drp = EI("d_rep", [128, 16]); nwc = EI("nw_c", [128, 16])
    tmk = EI("tmask", [128, 384])
    gate = EI("gate_w", [8, D]); b1c = EI("b1_c", [8, 128, 32])
    b2r = EI("b2_r", [8, 1, D]); w1 = EI("w1", [4096, 8, D]); w2 = EI("w2", [D, 8, 4096])
    wq = EI("Wq", [D, D]); wk = EI("Wk", [D, D]); wv = EI("Wv", [D, D])
    bqc = EI("bq_c", [128, 8]); bkc = EI("bk_c", [128, 8]); bvr = EI("bv_r", [1, D])
    lqk = EI("lqk", [16, 256]); c16 = EI("c16x8", [16, 8]); cB = EI("cB16", [16, 128])
    gnw = EI("gnw_c", [64, 16]); gnb = EI("gnb_c", [64, 16])
    wo = EI("Wo", [D, D]); bor = EI("bo_r", [1, D])
    ws1 = EI("Ws1", [D, 4096]); bs1c = EI("bs1_c", [128, 32])
    ws2 = EI("Ws2", [2048, D]); bs2r = EI("bs2_r", [1, D])
    we1 = EI("We1", [12, D, 6144]); be1c = EI("be1_c", [12, 128, 48])
    we2 = EI("We2", [12, 3072, D]); be2r = EI("be2_r", [12, 1, D])
    wr = EI("Wr", [D, 12]); brr = EI("br_r", [1, 12])
    xn0 = IN("xn0_fm", [D, L], BF16)
    s0 = IN("s0_tm", [L, D])
    x1 = IN("x1_tm", [L, D])
    xn1 = IN("xn1_fm", [D, L], BF16)
    qf = IN("q_fm", [D, L], BF16); kf = IN("k_fm", [D, L], BF16)
    vt = IN("v_tm", [L, D], BF16)
    of = IN("o_fm", [D, L], BF16)
    s1 = IN("s1_tm", [L, D])
    g0acc = IN("g0acc", [L, D]); g1acc = IN("g1acc", [L, D])
    mrow = IN("mrow", [32, 128])
    w1b = IN("w1b", [4096, 8, D], BF16); w2b = IN("w2b", [D, 8, 4096], BF16)
    we1b = IN("we1b", [12, D, 6144], BF16); we2b = IN("we2b", [12, 3072, D], BF16)
    CAP = 1024
    xn1tb = IN("xn1_tmb", [L, D], BF16)
    xcomp = IN("xcomp", [12 * CAP, D], BF16)
    yewt = IN("yew", [12 * CAP, D], F32)
    wcomp = IN("wcomp", [12 * CAP, 1], F32)
    out = nc.dram_tensor("out_tm", [L, D], F32, kind="ExternalOutput").ap()
    with stack:
        ctx = Ctx(nc, stack)
        ctx.emit_const_load(cc)
        with ctx.scope():
            emit_ln_fm(ctx, u, xn0, cidb, "n0")
        ctx.phase_barrier()
        with ctx.scope():
            emit_mamba(ctx, "p1", xn0, s0, ipw, opw, cwc, cbc, dtb, alg, drp,
                       nwc, tmk, cidb, cidf)
            for e in range(8):
                ctx.op("gpsimd", lambda g, e=e: g.dma_start(out=w1b[:, e, :], in_=w1[:, e, :]),
                       inc=(("pcw", 16),))
            for e in range(8):
                ctx.op("gpsimd", lambda g, e=e: g.dma_start(out=w2b[:, e, :], in_=w2[:, e, :]),
                       inc=(("pcw", 16),))
        ctx.phase_barrier()
        with ctx.scope():
            emit_l0moe(ctx, "p2", xn0, u, s0, x1, g0acc, gate, b1c, b2r, w1b, w2b,
                       cidb, onesrow, wload_wait=[("pcw", 256)])
            for e in range(12):
                ctx.op("gpsimd", lambda g, e=e: g.dma_start(out=we1b[e], in_=we1[e]),
                       inc=(("pcw", 16),))
            for e in range(12):
                ctx.op("gpsimd", lambda g, e=e: g.dma_start(out=we2b[e], in_=we2[e]),
                       inc=(("pcw", 16),))
        ctx.phase_barrier()
        with ctx.scope():
            emit_ln_fm(ctx, x1, xn1, cidb, "n1")
        ctx.phase_barrier()
        with ctx.scope():
            emit_qkv(ctx, "p3", xn1, wq, wk, wv, bqc, bkc, lqk, c16, cB,
                     onesrow, bvr, qf, kf, vt)
        ctx.phase_barrier()
        with ctx.scope():
            emit_attn(ctx, "p4", qf, kf, vt, gnw, gnb, cidf, onesrow, mrow, of)
        ctx.phase_barrier()
        with ctx.scope():
            emit_proj_tm(ctx, "p5", of, wo, bor, s1, D, D, onesrow)
        ctx.phase_barrier()
        with ctx.scope():
            emit_l1ffn(ctx, "p6", xn1, x1, s1, out, g1acc, ws1, bs1c, bs2r, ws2,
                       we1b, be1c, we2b, be2r, wr, brr, onesrow,
                       wload_wait=[("pcw", 640)], cid_f32=cidf, cid_bf=cidb,
                       sparse=None)
        ctx.assemble()
    return nc, ctx


def fused_shared_inputs(inputs, consts):
    lq1, lk1 = _f32(inputs["L1_attn_lq1"]), _f32(inputs["L1_attn_lk1"])
    lq2, lk2 = _f32(inputs["L1_attn_lq2"]), _f32(inputs["L1_attn_lk2"])
    sh = {
        "cid_bf": consts["cid_bf"], "cid_f32": consts["cid_f32"],
        "onesrow": consts["onesrow"], "c16x8": consts["c16x8"], "cB16": consts["cB16"],
        "gate_w": _f32(inputs["L0_ffn_gate_w"]),
        "b1_c": np.stack([col_layout(_f32(inputs["L0_ffn_b1"][e])) for e in range(8)]),
        "b2_r": _f32(inputs["L0_ffn_b2"])[:, None, :],
        "w1": _f32(inputs["L0_ffn_w1"]), "w2": _f32(inputs["L0_ffn_w2"]),
        "Wq": _f32(inputs["L1_attn_Wq"]), "Wk": _f32(inputs["L1_attn_Wk"]),
        "Wv": _f32(inputs["L1_attn_Wv"]),
        "bq_c": col_layout(_f32(inputs["L1_attn_bq"])),
        "bk_c": col_layout(_f32(inputs["L1_attn_bk"])),
        "bv_r": _f32(inputs["L1_attn_bv"])[None, :],
        "lqk": np.concatenate([lq1, lk1, lq2, lk2], axis=1),
        "gnw_c": np.ascontiguousarray(_f32(inputs["L1_attn_gn_w"]).T),
        "gnb_c": np.ascontiguousarray(_f32(inputs["L1_attn_gn_b"]).T),
        "Wo": _f32(inputs["L1_attn_Wo"]), "bo_r": _f32(inputs["L1_attn_bo"])[None, :],
        "Ws1": _f32(inputs["L1_ffn_Ws1"]), "bs1_c": col_layout(_f32(inputs["L1_ffn_bs1"])),
        "Ws2": _f32(inputs["L1_ffn_Ws2"]), "bs2_r": _f32(inputs["L1_ffn_bs2"])[None, :],
        "We1": _f32(inputs["L1_ffn_We1"]),
        "be1_c": np.stack([col_layout(_f32(inputs["L1_ffn_be1"][e])) for e in range(12)]),
        "We2": _f32(inputs["L1_ffn_We2"]), "be2_r": _f32(inputs["L1_ffn_be2"])[:, None, :],
        "Wr": _f32(inputs["L1_ffn_Wr"]), "br_r": _f32(inputs["L1_ffn_br"])[None, :],
    }
    sh.update(p1_host_inputs(inputs, consts))
    return sh


def kernel(**inputs):
    consts = host_constants()
    u_full = _f32(inputs["L0_attn_u"])
    cores = list(range(NCORES))
    nc, ctx = build_fused()
    sh = fused_shared_inputs(inputs, consts)
    sh["cc"] = ctx.const_array()
    maps = [dict(sh, u=u_full[i]) for i in cores]
    res = run_bass_kernel_spmd(nc, maps, core_ids=cores)
    return np.stack([res.results[i]["out_tm"] for i in cores]).astype(np.float32)


def kernel_unfused(**inputs):
    consts = host_constants()
    u_full = _f32(inputs["L0_attn_u"])
    cores = list(range(NCORES))

    def run(nc, ctx, shared, per_core):
        maps = []
        for i in cores:
            m = dict(shared)
            if ctx is not None:
                m["cc"] = ctx.const_array()
            for k, v in per_core.items():
                m[k] = v[i]
            maps.append(m)
        return run_bass_kernel_spmd(nc, maps, core_ids=cores).results

    nc, ctx = build_ln("u", "xn_fm")
    r = run(nc, ctx, {"cid_bf": consts["cid_bf"]}, {"u": u_full})
    xn0 = [r[i]["xn_fm"] for i in cores]

    nc, ctx = build_p1()
    p1h = p1_host_inputs(inputs, consts)
    r = run(nc, ctx, p1h, {"xn_fm": np.stack(xn0)})
    s0 = [r[i]["s0_tm"] for i in cores]

    nc, ctx = build_p2()
    shared = {
        "gate_w": _f32(inputs["L0_ffn_gate_w"]),
        "b1_c": np.stack([col_layout(_f32(inputs["L0_ffn_b1"][e])) for e in range(8)]),
        "b2_r": _f32(inputs["L0_ffn_b2"])[:, None, :],
        "w1": _f32(inputs["L0_ffn_w1"]), "w2": _f32(inputs["L0_ffn_w2"]),
        "cid_bf": consts["cid_bf"], "onesrow": consts["onesrow"],
    }
    r = run(nc, None, shared, {"xn_fm": np.stack(xn0), "u_tm": u_full,
                               "s0_tm": np.stack(s0)})
    x1 = [r[i]["x1_tm"] for i in cores]

    nc, ctx = build_ln("x1", "xn1_fm")
    r = run(nc, ctx, {"cid_bf": consts["cid_bf"]}, {"x1": np.stack(x1)})
    xn1 = [r[i]["xn1_fm"] for i in cores]

    nc = build_p3()
    lq1, lk1 = _f32(inputs["L1_attn_lq1"]), _f32(inputs["L1_attn_lk1"])
    lq2, lk2 = _f32(inputs["L1_attn_lq2"]), _f32(inputs["L1_attn_lk2"])
    shared = {
        "Wq": _f32(inputs["L1_attn_Wq"]), "Wk": _f32(inputs["L1_attn_Wk"]),
        "Wv": _f32(inputs["L1_attn_Wv"]),
        "bq_c": col_layout(_f32(inputs["L1_attn_bq"])),
        "bk_c": col_layout(_f32(inputs["L1_attn_bk"])),
        "bv_r": _f32(inputs["L1_attn_bv"])[None, :],
        "lqk": np.concatenate([lq1, lk1, lq2, lk2], axis=1),
        "c16x8": consts["c16x8"], "cB16": consts["cB16"], "onesrow": consts["onesrow"],
    }
    r = run(nc, None, shared, {"xn_fm": np.stack(xn1)})
    qf = [r[i]["q_fm"] for i in cores]
    kf = [r[i]["k_fm"] for i in cores]
    vt = [r[i]["v_tm"] for i in cores]

    nc, ctx = build_p4()
    shared = {
        "gnw_c": np.ascontiguousarray(_f32(inputs["L1_attn_gn_w"]).T),
        "gnb_c": np.ascontiguousarray(_f32(inputs["L1_attn_gn_b"]).T),
        "cid_f32": consts["cid_f32"], "onesrow": consts["onesrow"],
    }
    r = run(nc, ctx, shared, {"q_fm": np.stack(qf), "k_fm": np.stack(kf),
                              "v_tm": np.stack(vt)})
    of = [r[i]["o_fm"] for i in cores]

    nc = build_p5()
    shared = {"Wo": _f32(inputs["L1_attn_Wo"]),
              "bo_r": _f32(inputs["L1_attn_bo"])[None, :],
              "onesrow": consts["onesrow"]}
    r = run(nc, None, shared, {"o_fm": np.stack(of)})
    s1 = [r[i]["s1_tm"] for i in cores]

    nc = build_p6()
    shared = {
        "Ws1": _f32(inputs["L1_ffn_Ws1"]), "bs1_c": col_layout(_f32(inputs["L1_ffn_bs1"])),
        "Ws2": _f32(inputs["L1_ffn_Ws2"]), "bs2_r": _f32(inputs["L1_ffn_bs2"])[None, :],
        "We1": _f32(inputs["L1_ffn_We1"]),
        "be1_c": np.stack([col_layout(_f32(inputs["L1_ffn_be1"][e])) for e in range(12)]),
        "We2": _f32(inputs["L1_ffn_We2"]), "be2_r": _f32(inputs["L1_ffn_be2"])[:, None, :],
        "Wr": _f32(inputs["L1_ffn_Wr"]), "br_r": _f32(inputs["L1_ffn_br"])[None, :],
        "onesrow": consts["onesrow"],
    }
    r = run(nc, None, shared, {"xn_fm": np.stack(xn1), "x1_tm": np.stack(x1),
                               "s1_tm": np.stack(s1)})
    out = np.stack([r[i]["out_tm"] for i in cores]).astype(np.float32)
    return out
